# version 11
# baseline (speedup 1.0000x reference)
import numpy as np
import ml_dtypes
import concourse.bass as bass
import concourse.mybir as mybir
from concourse.bass_utils import run_bass_kernel_spmd

F32 = mybir.dt.float32
BF16 = mybir.dt.bfloat16
AF = mybir.ActivationFunctionType
ALU = mybir.AluOpType
AX = mybir.AxisListType.X

S = 2048
DM = 2048
H = 16
KVH = 4
HD = 128
DE = 8192
NT = S // 128
ND = DM // 128
NE = DE // 128
EPS_LN = 1e-5
EPS_RMS = 1.1920928955078125e-07
EPS_TQ = 1e-8
RC = 12582912.0
SQRT_HD = float(np.sqrt(128.0))
NW_UG = float(DE * DM)
NW_D = float(DM * DE)

ENGINES = ("sync", "gpsimd", "tensor", "vector", "scalar")


class Ctr:
    __slots__ = ("h", "n")

    def __init__(self, h):
        self.h = h
        self.n = 0


class Prog:
    def __init__(self):
        self.ops = []

    def op(self, eng, fn, waits=(), inc=None, amt=1):
        incs = ()
        if inc is not None:
            inc.n += amt
            incs = ((inc.h, inc.n) if False else (inc.h, amt),)
        self.ops.append((eng, fn, tuple(waits), incs))

    def dma(self, eng, out, in_, waits=(), inc=None, **kw):
        kw = dict(kw)
        self.op(eng, (lambda e, o=out, i=in_, k=kw: e.dma_start(out=o, in_=i, **k)),
                waits, inc, 16)

    def wait(self, eng, waits):
        self.op(eng, None, waits)

    dma_registry = []

    def barrier(self, bar):
        for e in ENGINES:
            waits = [(c.h, c.n) for (en, c) in Prog.dma_registry if en == e]
            self.op(e, (lambda eng: eng.nop()), waits, bar, 1)
        tgt = bar.n
        for e in ENGINES:
            self.wait(e, [(bar.h, tgt)])

    def emit(self, block):
        for name in ENGINES:
            dec = getattr(block, name)

            def body(eng, _name=name):
                for (e, fn, waits, incs) in self.ops:
                    if e != _name:
                        continue
                    for (hh, v) in waits:
                        eng.wait_ge(hh, v)
                    if fn is None:
                        if incs:
                            inst = eng.nop()
                            for (hh, a) in incs:
                                inst.then_inc(hh, a)
                        continue
                    inst = fn(eng)
                    for (hh, a) in incs:
                        inst.then_inc(hh, a)
            dec(body)


def build_nc(stop_after="F2", dbg=()):
    import contextlib
    order = ["A1", "A2K", "A2V", "WO", "A3", "A4", "F1", "F2"]
    stage_n = order.index(stop_after)

    nc = bass.Bass("TRN2", target_bir_lowering=False)
    p = Prog()

    x_h = nc.dram_tensor("x", [S, DM], F32, kind="ExternalInput").ap()
    wq_h = nc.dram_tensor("wq", [DM, DM], F32, kind="ExternalInput").ap()
    wk_h = nc.dram_tensor("wk", [512, DM], F32, kind="ExternalInput").ap()
    wv_h = nc.dram_tensor("wv", [512, DM], F32, kind="ExternalInput").ap()
    wo_h = nc.dram_tensor("wo", [DM, DM], F32, kind="ExternalInput").ap()
    gain_h = nc.dram_tensor("gain", [1, H], F32, kind="ExternalInput").ap()
    wu_h = nc.dram_tensor("wu", [DE, DM], F32, kind="ExternalInput").ap()
    wg_h = nc.dram_tensor("wg", [DE, DM], F32, kind="ExternalInput").ap()
    wd_h = nc.dram_tensor("wd", [DM, DE], F32, kind="ExternalInput").ap()
    identb_h = nc.dram_tensor("identb", [128, 128], BF16, kind="ExternalInput").ap()
    identf_h = nc.dram_tensor("identf", [128, 128], F32, kind="ExternalInput").ap()
    cosf_h = nc.dram_tensor("cosf", [128, S], BF16, kind="ExternalInput").ap()
    sinf_h = nc.dram_tensor("sinf", [128, S], BF16, kind="ExternalInput").ap()
    maskd_h = nc.dram_tensor("maskd", [128, 128], BF16, kind="ExternalInput").ap()
    maskf_h = nc.dram_tensor("maskf", [128, 128], BF16, kind="ExternalInput").ap()
    onescb_h = nc.dram_tensor("onescb", [128, 1], BF16, kind="ExternalInput").ap()
    onescf_h = nc.dram_tensor("onescf", [128, 1], F32, kind="ExternalInput").ap()
    onesrf_h = nc.dram_tensor("onesrf", [1, 128], F32, kind="ExternalInput").ap()
    sign_h = nc.dram_tensor("sign", [128, 1], F32, kind="ExternalInput").ap()
    epsc_h = nc.dram_tensor("epsc", [128, 1], F32, kind="ExternalInput").ap()
    out_h = nc.dram_tensor("out", [S, DM], F32, kind="ExternalOutput").ap()
    x1_t = nc.dram_tensor("x1_s", [S, DM], F32)
    x1_h = x1_t.ap()
    ln2t_t = nc.dram_tensor("ln2t_s", [4, DM, 512], BF16)
    wot_t = nc.dram_tensor("wot_s", [ND, DM, 128], BF16)
    wut_t = nc.dram_tensor("wut_s", [NE, 128, DM], BF16)
    wgt_t = nc.dram_tensor("wgt_s", [NE, 128, DM], BF16)
    wdt_t = nc.dram_tensor("wdt_s", [ND, 128, DE], BF16)
    out_t = out_h.tensor if hasattr(out_h, "tensor") else None

    dbg_tensors = {}
    tq_hold = {}

    def dbg_out(name, shape, dtype):
        t = nc.dram_tensor("dbg_" + name, shape, dtype, kind="ExternalOutput").ap()
        dbg_tensors[name] = t
        return t

    es = contextlib.ExitStack()

    def sem(name):
        return Ctr(es.enter_context(nc.semaphore(name)))

    mm = sem("mm")
    mm2 = sem("mm2")
    vv = sem("vv")
    ss = sem("ss")
    vself = sem("vself")
    sself = sem("sself")
    bar = sem("bar")
    c_ld = sem("c_ld")
    d_x = [sem("d_x0"), sem("d_x1")]
    d_w = [sem("d_w0"), sem("d_w1")]
    d_sw = sem("d_sw")
    d_sc = [sem("d_sc0"), sem("d_sc1"), sem("d_sc2"), sem("d_sc3")]
    d_wot = [sem("d_wot0"), sem("d_wot1")]
    d_wotl = [sem("d_wotl0"), sem("d_wotl1")]
    d_x1p = sem("d_x1p")
    d_acc = [sem("d_acc0"), sem("d_acc1")]
    d_x1l = [sem("d_x1l0"), sem("d_x1l1")]
    d_ln2 = [sem("d_ln20"), sem("d_ln21")]
    d_ln2l = sem("d_ln2l")
    d_outp = sem("d_outp")
    d_fo = [sem("d_fo0"), sem("d_fo1")]
    d_fw = [sem(f"d_fw{i}") for i in range(4)]
    d_tc = [sem("d_tc0"), sem("d_tc1"), sem("d_tc2")]
    d_tcl = [sem(f"d_tcl{i}") for i in range(4)]

    Prog.dma_registry = (
        [("gpsimd", c) for c in (c_ld, d_x[0], d_x[1], d_w[0], d_w[1], d_sw,
                                 d_wot[0], d_wot[1], d_x1p, d_acc[0], d_acc[1],
                                 d_ln2[0], d_ln2[1], d_outp, d_fo[0], d_fo[1],
                                 d_tc[0], d_tc[1], d_tc[2])] +
        [("sync", c) for c in (d_sc[0], d_sc[1], d_sc[2], d_sc[3],
                               d_wotl[0], d_wotl[1],
                               d_x1l[0], d_x1l[1], d_ln2l, d_fw[0], d_fw[1],
                               d_fw[2], d_fw[3], d_tcl[0], d_tcl[1],
                               d_tcl[2], d_tcl[3])])

    def sb_in(stack, name, shape, dtype):
        return stack.enter_context(nc.sbuf_tensor(name, shape, dtype))

    def ps_in(stack, name, shape):
        return stack.enter_context(nc.psum_tensor(name, shape, F32))

    identb = sb_in(es, "sb_identb", [128, 128], BF16)
    identf = sb_in(es, "sb_identf", [128, 128], F32)
    cosf = sb_in(es, "sb_cosf", [128, S], BF16)
    sinf = sb_in(es, "sb_sinf", [128, S], BF16)
    maskd = sb_in(es, "sb_maskd", [128, 128], BF16)
    maskf = sb_in(es, "sb_maskf", [128, 128], BF16)
    onescb = sb_in(es, "sb_onescb", [128, 1], BF16)
    onescf = sb_in(es, "sb_onescf", [128, 1], F32)
    onesrf = sb_in(es, "sb_onesrf", [1, 128], F32)
    signc = sb_in(es, "sb_signc", [128, 1], F32)
    epsc = sb_in(es, "sb_epsc", [128, 1], F32)
    gainr = sb_in(es, "sb_gainr", [1, H], F32)
    scaleT = sb_in(es, "scaleT", [128, H], F32)
    scaleTA = sb_in(es, "scaleTA", [128, H], F32)
    sgnT = sb_in(es, "sgnT", [128, H], F32)
    biasT = sb_in(es, "biasT", [128, H], F32)
    tqt = sb_in(es, "tqt", [128, 8], F32)
    tq_acc = sb_in(es, "tq_acc", [128, 8], F32)
    tq_row = sb_in(es, "tq_row", [1, 8], F32)

    for dst, src in ((identb, identb_h), (identf, identf_h), (cosf, cosf_h),
                     (sinf, sinf_h), (maskd, maskd_h), (maskf, maskf_h),
                     (onescb, onescb_h), (onescf, onescf_h), (onesrf, onesrf_h),
                     (signc, sign_h), (epsc, epsc_h), (gainr, gain_h)):
        p.dma("gpsimd", dst[:], src[:], inc=c_ld)
    consts_rdy = (c_ld.h, c_ld.n)
    for i in range(4):
        p.dma("gpsimd", x1_h[i * 512:(i + 1) * 512, :],
              x_h[i * 512:(i + 1) * 512, :], inc=d_x1p)
    x1p_rdy = (d_x1p.h, d_x1p.n)

    gb_ps_guard = contextlib.ExitStack()
    gb_ps = ps_in(gb_ps_guard, "gb_ps", [128, 16])
    p.op("tensor", (lambda e: e.matmul(gb_ps[:], onesrf[:], gainr[:],
                                       start=True, stop=True)), [consts_rdy], mm2)
    p.op("vector", (lambda e: e.tensor_scalar(scaleT[:], gb_ps[:], 1.0 / SQRT_HD,
                                              None, ALU.mult)), [(mm2.h, mm2.n)], vv)
    p.op("scalar", (lambda e: e.activation(biasT[:], gb_ps[:], AF.Abs)),
         [(mm2.h, mm2.n)], sself)
    p.op("scalar", (lambda e: e.mul(biasT[:], biasT[:], -SQRT_HD)),
         [(sself.h, sself.n)], ss)
    p.op("scalar", (lambda e: e.activation(scaleTA[:], scaleT[:], AF.Abs)),
         [(vv.h, vv.n)], ss)
    p.op("vector", (lambda e: e.tensor_scalar(sgnT[:], scaleT[:], 0.0, None,
                                              ALU.is_ge)), [(vv.h, vv.n)], vv)
    p.op("vector", (lambda e: e.tensor_scalar(sgnT[:], sgnT[:], 2.0, -1.0,
                                              ALU.mult, ALU.add)),
         [(vv.h, vv.n)], vv)
    gain_rdy = (vv.h, vv.n)
    gain_rdy_s = (ss.h, ss.n)
    gb_ps_guard.close()

    f0 = {"i": 0, "buf_use": [None] * 4, "col5_use": None, "bufs": None}

    def f0_steps(n):
        import os as _os2
        if f0["bufs"] is None:
            return
        for _ in range(n):
            i = f0["i"]
            if i >= 192:
                return
            f0["i"] = i + 1
            if i < 64:
                w, hbm = 0, wu_h[i * 128:(i + 1) * 128, :]
            elif i < 128:
                w, hbm = 1, wg_h[(i - 64) * 128:(i - 63) * 128, :]
            else:
                r, piece = divmod(i - 128, 4)
                w, hbm = 2, wd_h[r * 128:(r + 1) * 128,
                                 piece * 2048:(piece + 1) * 2048]
            par = i % 4
            buf = f0["bufs"][par]
            wt = [f0["buf_use"][par]] if f0["buf_use"][par] else []
            p.dma("sync", buf[:], hbm, wt, d_sc[par])
            ld = (d_sc[par].h, d_sc[par].n)
            first = i in (0, 64, 128)
            acc = tq_acc[:, w:w + 1] if first else tq_acc[:, 5:6]
            wt = [ld]
            if f0["col5_use"] is not None:
                wt.append(f0["col5_use"])
            p.op("scalar", (lambda e, b=buf, a=acc: e.activation(
                b[:], b[:], AF.Abs, accum_out=a)), wt, sself)
            f0["buf_use"][par] = (sself.h, sself.n)
            if not first:
                p.op("vector", (lambda e, w=w: e.tensor_add(
                    tq_acc[:, w:w + 1], tq_acc[:, w:w + 1], tq_acc[:, 5:6])),
                    [(sself.h, sself.n)] + (
                        [(vself.h, vself.n)] if f0["col5_use"] else []), vself)
                f0["col5_use"] = (vself.h, vself.n)
            else:
                f0["col5_use"] = None

    esY = contextlib.ExitStack()
    y_t = [sb_in(esY, f"y{i}", [128, S], BF16) for i in range(H)]

    esA = contextlib.ExitStack()
    lnT = sb_in(esA, "lnT", [128, NT * S], BF16)
    lnT3 = lnT.ap().rearrange("p (j s) -> p j s", j=ND)
    k_fm = [sb_in(esA, f"kfm{i}", [128, S], BF16) for i in range(KVH)]
    v_tm = [sb_in(esA, f"vtm{i}", [128, 512], BF16) for i in range(NT)]
    wraw = [sb_in(esA, "wraw0", [128, DM], BF16)]
    wraw_use = [None]

    esTP = contextlib.ExitStack()
    tps = [ps_in(esTP, f"tp{i}", [128, 512]) for i in range(2)]

    tp_guard = {}

    def emit_transpose16(src_sb, dst_copy_fns, first_wait, ident, mmctr):
        for g in range(4):
            tp = tps[g % 2]
            wt = list(first_wait) if g == 0 else []
            gg = tp_guard.get(id(tp))
            if gg is not None:
                wt.append(gg)
            for jj in range(4):
                j = g * 4 + jj
                p.op("tensor", (lambda e, j=j, jj=jj, tp=tp, s=src_sb, idn=ident:
                     e.matmul(tp[:, jj * 128:(jj + 1) * 128],
                              s[:, j * 128:(j + 1) * 128], idn[:],
                              start=True, stop=True)),
                     wt if jj == 0 else (), mmctr if jj == 3 else None)
            p.op("vector", (lambda e, g=g, tp=tp, f=dst_copy_fns: f(e, g, tp)),
                 [(mmctr.h, mmctr.n)], vv)
            tp_guard[id(tp)] = (vv.h, vv.n)
        return (vv.h, vv.n)

    esA1 = contextlib.ExitStack()
    xbuf = [sb_in(esA1, f"xbuf{i}", [128, DM], F32) for i in range(2)]
    lnbuf = [sb_in(esA1, "lnbuf0", [128, DM], BF16)]
    junk1 = sb_in(esA1, "junk1", [128, 1024], F32)
    st1 = sb_in(esA1, "st1", [128, 8], F32)

    def emit_ln(xsrc, lnb, stats, junk, wait_x, norm_extra=()):
        p.op("vector", (lambda e, x=xsrc: e.reduce_sum(stats[:, 0:1], x[:], AX)),
             wait_x, vself)
        p.op("scalar", (lambda e, x=xsrc: e.activation(
            junk[:], x[:, 0:1024], AF.Square, accum_out=stats[:, 1:2])),
            wait_x, sself)
        p.op("scalar", (lambda e, x=xsrc: e.activation(
            junk[:], x[:, 1024:2048], AF.Square, accum_out=stats[:, 2:3])),
            (), sself)
        sq_done = (sself.h, sself.n)
        sum_done = (vself.h, vself.n)
        p.op("vector", (lambda e: e.tensor_scalar(stats[:, 3:4], stats[:, 0:1],
                                                  1.0 / DM, None, ALU.mult)),
             [sq_done, sum_done], vself)
        p.op("vector", (lambda e: e.tensor_add(stats[:, 4:5], stats[:, 1:2],
                                               stats[:, 2:3])), (), vself)
        p.op("vector", (lambda e: e.tensor_scalar(stats[:, 4:5], stats[:, 4:5],
                                                  1.0 / DM, EPS_LN, ALU.mult,
                                                  ALU.add)),
             [(vself.h, vself.n)], vself)
        p.op("vector", (lambda e: e.tensor_mul(stats[:, 7:8], stats[:, 3:4],
                                               stats[:, 3:4])),
             [(vself.h, vself.n)], vself)
        p.op("vector", (lambda e: e.tensor_sub(stats[:, 4:5], stats[:, 4:5],
                                               stats[:, 7:8])),
             [(vself.h, vself.n)], vself)
        p.op("scalar", (lambda e: e.activation(stats[:, 5:6], stats[:, 4:5],
                                               AF.Ln)), [(vself.h, vself.n)], sself)
        p.op("scalar", (lambda e: e.activation(stats[:, 5:6], stats[:, 5:6],
                                               AF.Exp, scale=-0.5)),
             [(sself.h, sself.n)], sself)
        p.op("scalar", (lambda e: e.activation(stats[:, 6:7], stats[:, 3:4],
                                               AF.Identity, scale=stats[:, 5:6])),
             [(sself.h, sself.n)], sself)
        p.op("scalar", (lambda e: e.mul(stats[:, 6:7], stats[:, 6:7], -1.0)),
             [(sself.h, sself.n)], sself)
        p.op("scalar", (lambda e, x=xsrc, o=lnb: e.activation(
            o[:], x[:], AF.Identity, bias=stats[:, 6:7], scale=stats[:, 5:6])),
            [(sself.h, sself.n)] + list(norm_extra), ss)
        return (ss.h, ss.n)

    wotmp = [sb_in(esA1, f"wotmp{i}", [128, DM], BF16) for i in range(2)]
    wo_state = {"m": 0, "use": [None, None], "written": []}

    def wo_step():
        m = wo_state["m"]
        if m >= ND:
            return
        wo_state["m"] = m + 1
        par = m % 2
        wt = [wraw_use[0]] if wraw_use[0] else [consts_rdy]
        p.dma("gpsimd", wraw[0][:], wo_h[m * 128:(m + 1) * 128, :], wt, d_w[0])
        ld = (d_w[0].h, d_w[0].n)

        def cpo(e, g, tp, w=wotmp[par]):
            return e.tensor_copy(w[:, g * 512:(g + 1) * 512], tp[:])
        trw = [ld] + ([wo_state["use"][par]] if wo_state["use"][par] else [])
        tr = emit_transpose16(wraw[0], cpo, trw, identb, mm2)
        wraw_use[0] = tr
        p.dma("gpsimd",
              bass.AP(wot_t, m * DM * 128, [[128, 128], [128 * 128, 16],
                                            [1, 128]]),
              wotmp[par][:], [tr], d_wot[par])
        wo_state["use"][par] = (d_wot[par].h, d_wot[par].n)
        wo_state["written"].append((d_wot[par].h, d_wot[par].n))

    a1_norm = []
    a1_tp = []
    for t in range(NT):
        par = t % 2
        gate = [a1_norm[t - 2]] if t >= 2 else [consts_rdy]
        p.dma("gpsimd", xbuf[par][:], x_h[t * 128:(t + 1) * 128, :], gate, d_x[par])
        ld = (d_x[par].h, d_x[par].n)
        norm_extra = [a1_tp[t - 1]] if t >= 1 else ()
        nd = emit_ln(xbuf[par], lnbuf[0], st1, junk1, [ld], norm_extra)
        a1_norm.append(nd)

        def cp(e, g, tp, t=t):
            return e.tensor_copy(lnT3[:, g * 4:g * 4 + 4, t * 128:(t + 1) * 128],
                          tp.ap().rearrange("p (a b) -> p a b", a=4)[:, :, :])
        fw = [nd, gain_rdy] if t == 0 else [nd]
        a1_tp.append(emit_transpose16(lnbuf[0], cp, fw, identb, mm2))
        wo_step()
    wot_all = [(d_wot[0].h, d_wot[0].n), (d_wot[1].h, d_wot[1].n)]
    esA1.close()
    lnT_rdy = (vv.h, vv.n)

    if "lnT" in dbg:
        d = dbg_out("lnT", [128, NT * S], BF16)
        p.dma("gpsimd", d[:], lnT[:], [lnT_rdy], c_ld)

    kv_rdy = None
    if stage_n >= 1:
        esK = contextlib.ExitStack()
        wT = sb_in(esK, "wT", [128, DM], BF16)
        khat = sb_in(esK, "khat", [128, S], BF16)
        khat_sw = sb_in(esK, "khat_sw", [128, S], BF16)
        sqb = [sb_in(esK, "sqb0", [128, 512], BF16)]
        rrow = [sb_in(esK, "rrow0", [1, 512], F32)]
        rstdB = [sb_in(esK, f"rstdB{i}", [128, 512], F32) for i in range(2)]
        esKp = contextlib.ExitStack()
        k_ps = ps_in(esKp, "k_ps", [128, S])
        sum_ps = ps_in(esKp, "sum_ps", [128, 512])
        bc_ps = ps_in(esKp, "bc_ps", [128, 512])

        def emit_rms(src_ps, n_cols, sq_pair, row_pair, rB_pair, mm_wait,
                     prev_use):
            nch = n_cols // 512
            sums_use = None
            bc_use = None
            row_use = [None]
            for ncI in range(nch):
                a, b = ncI * 512, (ncI + 1) * 512
                sq = sq_pair[0]
                row = row_pair[0]
                wt = [mm_wait]
                if sq_pair_use[0] is not None:
                    wt.append(sq_pair_use[0])
                p.op("scalar", (lambda e, a=a, b=b, sq=sq: e.activation(
                    sq[:], src_ps[:, a:b], AF.Square)), wt, ss)
                wt = [(ss.h, ss.n)]
                if sums_use is not None:
                    wt.append(sums_use)
                p.op("tensor", (lambda e, sq=sq: e.matmul(
                    sum_ps[0:1, :], onescb[:], sq[:], start=True, stop=True)),
                    wt, mm2)
                sq_pair_use[0] = (mm2.h, mm2.n)
                wt = [(mm2.h, mm2.n)]
                if row_use[0] is not None:
                    wt.append(row_use[0])
                p.op("scalar", (lambda e, row=row: e.activation(
                    row[:], sum_ps[0:1, :], AF.Ln, scale=1.0 / HD,
                    bias=epsc[0:1, :])), wt, sself)
                sums_use = (sself.h, sself.n)
                p.op("scalar", (lambda e, row=row: e.activation(
                    row[:], row[:], AF.Exp, scale=-0.5)),
                    [(sself.h, sself.n)], sself)
                wt = [(sself.h, sself.n)]
                if bc_use is not None:
                    wt.append(bc_use)
                p.op("tensor", (lambda e, row=row: e.matmul(
                    bc_ps[:], onesrf[:], row[:], start=True, stop=True)),
                    wt, mm2)
                row_use[0] = (mm2.h, mm2.n)
                p.op("scalar", (lambda e, rB=rB_pair[ncI % 2]: e.copy(
                    rB[:], bc_ps[:])), [(mm2.h, mm2.n)], ss)
                bc_use = (ss.h, ss.n)
                wt = [(ss.h, ss.n)]
                if prev_use is not None and ncI == 0:
                    wt.append(prev_use)
                p.op("vector", (lambda e, a=a, b=b, rB=rB_pair[ncI % 2]:
                     e.tensor_mul(khat[:, a:b], src_ps[:, a:b], rB[:])), wt, vv)
            return (vv.h, vv.n)

        def emit_rope(n_cols, dst, norm_done):
            p.dma("gpsimd", khat_sw[0:64, :n_cols], khat[64:128, :n_cols],
                  [norm_done], d_sw)
            p.dma("gpsimd", khat_sw[64:128, :n_cols], khat[0:64, :n_cols],
                  (), d_sw)
            swd = (d_sw.h, d_sw.n)
            p.op("vector", (lambda e, n=n_cols: e.tensor_mul(
                khat[:, :n], khat[:, :n], cosf[:, :n])), [swd], vv)
            p.op("vector", (lambda e, n=n_cols: e.tensor_mul(
                khat_sw[:, :n], khat_sw[:, :n], sinf[:, :n])), (), vv)
            p.op("vector", (lambda e, n=n_cols, d=dst: e.scalar_tensor_tensor(
                d[:, :n], khat_sw[:, :n], signc[:], khat[:, :n],
                ALU.mult, ALU.add)), [(vv.h, vv.n)], vv)
            return (vv.h, vv.n)

        sq_pair_use = [None]
        k_ps_use = None
        for c in range(KVH):
            par = 0
            wt = [wraw_use[par]] if wraw_use[par] else [lnT_rdy]
            p.dma("gpsimd", wraw[par][:], wk_h[c * 128:(c + 1) * 128, :],
                  wt, d_w[par])
            ld = (d_w[par].h, d_w[par].n)

            def cpw(e, g, tp):
                return e.tensor_copy(wT[:, g * 512:(g + 1) * 512], tp[:])
            tr = emit_transpose16(wraw[par], cpw, [ld], identb, mm2)
            wraw_use[par] = tr
            for j in range(ND):
                for ncI in range(4):
                    wt = []
                    if j == 0 and ncI == 0:
                        wt = [tr]
                        if k_ps_use is not None:
                            wt.append(k_ps_use)
                    p.op("tensor", (lambda e, j=j, ncI=ncI: e.matmul(
                        k_ps[:, ncI * 512:(ncI + 1) * 512],
                        wT[:, j * 128:(j + 1) * 128],
                        lnT3[:, j, ncI * 512:(ncI + 1) * 512],
                        start=(j == 0), stop=(j == ND - 1))),
                        wt, mm if (j == ND - 1 and ncI == 3) else None)
            nd = emit_rms(k_ps, S, sqb, rrow, rstdB, (mm.h, mm.n), None)
            k_ps_use = nd
            emit_rope(S, k_fm[c], nd)
        esKp.close()
        esK.close()
        kv_rdy = (vv.h, vv.n)

        if "kfm" in dbg:
            d = dbg_out("kfm", [128, KVH * S], BF16)
            for c in range(KVH):
                p.dma("gpsimd", d[:, c * S:(c + 1) * S], k_fm[c][:], [kv_rdy], c_ld)

    if stage_n >= 2:
        esV = contextlib.ExitStack()
        wvt = sb_in(esV, "wvt", [128, DE], BF16)
        wvt3 = wvt.ap().rearrange("p (j c) -> p j c", j=ND)
        esVp = contextlib.ExitStack()
        v_ps = [ps_in(esVp, f"v_ps{i}", [128, 512]) for i in range(2)]
        for c in range(KVH):
            par = 0
            wt = [wraw_use[par]] if wraw_use[par] else []
            p.dma("gpsimd", wraw[par][:], wv_h[c * 128:(c + 1) * 128, :],
                  wt, d_w[par])
            ld = (d_w[par].h, d_w[par].n)

            def cpv(e, g, tp, c=c):
                return e.tensor_copy(wvt3[:, g * 4:g * 4 + 4, c * 128:(c + 1) * 128],
                              tp.ap().rearrange("p (a b) -> p a b", a=4)[:, :, :])
            wraw_use[par] = emit_transpose16(wraw[par], cpv, [ld], identb, mm2)
        wvt_rdy = (vv.h, vv.n)
        v_done = []
        for t in range(NT):
            for j in range(ND):
                wt = []
                if j == 0:
                    wt = [wvt_rdy]
                    if t >= 2:
                        wt.append(v_done[t - 2])
                p.op("tensor", (lambda e, t=t, j=j: e.matmul(
                    v_ps[t % 2][:], lnT3[:, j, t * 128:(t + 1) * 128],
                    wvt3[:, j, :], start=(j == 0), stop=(j == ND - 1))),
                    wt, mm if j == ND - 1 else None)
            p.op("scalar", (lambda e, t=t: e.copy(v_tm[t][:], v_ps[t % 2][:])),
                 [(mm.h, mm.n)], ss)
            v_done.append((ss.h, ss.n))
        esVp.close()
        esV.close()

        if "vtm" in dbg:
            d = dbg_out("vtm", [128, NT * 512], BF16)
            for t in range(NT):
                p.dma("gpsimd", d[:, t * 512:(t + 1) * 512], v_tm[t][:],
                      [(ss.h, ss.n)], c_ld)

    if stage_n >= 3:
        pass

    if stage_n >= 4:
        esTP.close()
        esQ = contextlib.ExitStack()
        wTq = sb_in(esQ, "wTq", [128, DM], BF16)
        qhat = sb_in(esQ, "qhat", [128, S], BF16)
        qhat_sw = sb_in(esQ, "qhat_sw", [128, S], BF16)
        qsqb = [sb_in(esQ, "qsqb0", [128, 512], BF16)]
        qrow = [sb_in(esQ, "qrow0", [1, 512], F32)]
        qrB = [sb_in(esQ, "qrB0", [128, 512], F32)]
        q_fm = [sb_in(esQ, f"qfm{i}", [128, S], BF16) for i in range(2)]
        pT = [sb_in(esQ, f"pT{i}", [128, 640], BF16) for i in range(2)]
        recipb = [sb_in(esQ, f"recipb{i}", [1, 128], F32) for i in range(2)]
        rB_sb = [sb_in(esQ, f"rBsb{i}", [128, 128], F32) for i in range(2)]
        esQp = contextlib.ExitStack()
        q_ps = ps_in(esQp, "q_ps", [128, 1024])
        qsum_ps = ps_in(esQp, "qsum_ps", [128, 512])
        qbc_ps = ps_in(esQp, "qbc_ps", [128, 512])
        sc_ps = ps_in(esQp, "sc_ps", [128, 1024])
        yA = [ps_in(esQp, f"yA{i}", [128, 512]) for i in range(2)]

        st = {
            "qsq_use": [None],
            "qsum_use": None, "qbc_use": None,
            "q_ps_use": None, "wTq_use": None,
            "exp_prev": None, "av_prev": [None, None],
            "yA_use": [None, None],
            "recip_use": [None, None], "rB_use": [None, None],
            "qfm_use": [None, None],
        }

        def emit_qproj_mm(h):
            par = h % 2
            wpar = 0
            wt = [wraw_use[wpar]] if wraw_use[wpar] else []
            p.dma("gpsimd", wraw[wpar][:], wq_h[h * 128:(h + 1) * 128, :],
                  wt, d_w[wpar])
            ld = (d_w[wpar].h, d_w[wpar].n)
            for g in range(4):
                wt = [ld]
                if g == 0 and st["q_ps_use"] is not None:
                    wt.append(st["q_ps_use"])
                if g >= 1:
                    wt = [(vv.h, vv.n)]
                for jj in range(4):
                    j = g * 4 + jj
                    p.op("tensor", (lambda e, j=j, jj=jj: e.matmul(
                        q_ps[:, jj * 128:(jj + 1) * 128],
                        wraw[0][:, j * 128:(j + 1) * 128], identb[:],
                        start=True, stop=True)),
                        wt if jj == 0 else (), mm2 if jj == 3 else None)
                p.op("vector", (lambda e, g=g: e.tensor_copy(
                    wTq[:, g * 512:(g + 1) * 512], q_ps[:, 0:512])),
                    [(mm2.h, mm2.n)], vv)
            tr = (vv.h, vv.n)
            wraw_use[wpar] = tr
            for j in range(ND):
                for nn in range(2):
                    wt = [tr] if (j == 0 and nn == 0) else []
                    p.op("tensor", (lambda e, j=j, nn=nn: e.matmul(
                        q_ps[:, nn * 512:(nn + 1) * 512],
                        wTq[:, j * 128:(j + 1) * 128],
                        lnT3[:, j, nn * 512:(nn + 1) * 512],
                        start=(j == 0), stop=(j == ND - 1))),
                        wt, mm if j == ND - 1 else None)

        def qproj_rms_gen(h, out_holder):
            par = h % 2
            for half in range(2):
                if half == 1:
                    for j in range(ND):
                        for nn in range(2):
                            wt = [(vv.h, vv.n)] if (j == 0 and nn == 0) else []
                            p.op("tensor", (lambda e, j=j, nn=nn: e.matmul(
                                q_ps[:, nn * 512:(nn + 1) * 512],
                                wTq[:, j * 128:(j + 1) * 128],
                                lnT3[:, j, 1024 + nn * 512:
                                     1024 + (nn + 1) * 512],
                                start=(j == 0), stop=(j == ND - 1))),
                                wt, mm if j == ND - 1 else None)
                    yield
                mmw = (mm.h, mm.n)
                for nn in range(2):
                    base = half * 1024 + nn * 512
                    sq = qsqb[0]
                    wt = [mmw]
                    if st["qsq_use"][0] is not None:
                        wt.append(st["qsq_use"][0])
                    p.op("scalar", (lambda e, nn=nn, sq=sq: e.activation(
                        sq[:], q_ps[:, nn * 512:(nn + 1) * 512], AF.Square)),
                        wt, ss)
                    wt = [(ss.h, ss.n)]
                    if st["qsum_use"] is not None:
                        wt.append(st["qsum_use"])
                    p.op("tensor", (lambda e, sq=sq: e.matmul(
                        qsum_ps[0:1, :], onescb[:], sq[:],
                        start=True, stop=True)), wt, mm2)
                    st["qsq_use"][0] = (mm2.h, mm2.n)
                    yield
                    qr = qrow[0]
                    wt = [(mm2.h, mm2.n)]
                    if st.get("qrow_use", [None])[0] is not None:
                        wt.append(st["qrow_use"][0])
                    p.op("scalar", (lambda e, qr=qr: e.activation(
                        qr[:], qsum_ps[0:1, :], AF.Ln, scale=1.0 / HD,
                        bias=epsc[0:1, :])), wt, sself)
                    st["qsum_use"] = (sself.h, sself.n)
                    p.op("scalar", (lambda e, qr=qr: e.activation(
                        qr[:], qr[:], AF.Exp, scale=-0.5)),
                        [(sself.h, sself.n)], sself)
                    wt = [(sself.h, sself.n)]
                    if st["qbc_use"] is not None:
                        wt.append(st["qbc_use"])
                    p.op("tensor", (lambda e, qr=qr: e.matmul(
                        qbc_ps[:], onesrf[:], qr[:],
                        start=True, stop=True)), wt, mm2)
                    st.setdefault("qrow_use", [None])[0] = (mm2.h, mm2.n)
                    yield
                    wt2 = [(mm2.h, mm2.n)]
                    if st.get("qrB_use") is not None:
                        wt2.append(st["qrB_use"])
                    p.op("scalar", (lambda e: e.copy(qrB[0][:], qbc_ps[:])),
                         wt2, ss)
                    st["qbc_use"] = (ss.h, ss.n)
                    p.op("vector", (lambda e, b=base, nn=nn: e.tensor_mul(
                        qhat[:, b:b + 512], q_ps[:, nn * 512:(nn + 1) * 512],
                        qrB[0][:])), [(ss.h, ss.n)], vv)
                    st["qrB_use"] = (vv.h, vv.n)
                    yield
            nd = (vv.h, vv.n)
            st["q_ps_use"] = nd
            wt = [nd]
            if st["qfm_use"][par] is not None:
                wt.append(st["qfm_use"][par])
            p.dma("gpsimd", qhat_sw[0:64, :], qhat[64:128, :], wt, d_sw)
            p.dma("gpsimd", qhat_sw[64:128, :], qhat[0:64, :], (), d_sw)
            swd = (d_sw.h, d_sw.n)
            p.op("vector", (lambda e: e.tensor_mul(qhat[:], qhat[:], cosf[:])),
                 [swd], vv)
            p.op("vector", (lambda e: e.tensor_mul(qhat_sw[:], qhat_sw[:],
                                                   sinf[:])), (), vv)
            p.op("vector", (lambda e, d=q_fm[par]: e.scalar_tensor_tensor(
                d[:], qhat_sw[:], signc[:], qhat[:], ALU.mult, ALU.add)),
                [(vv.h, vv.n)], vv)
            p.op("vector", (lambda e, d=q_fm[par], h=h: e.tensor_scalar(
                d[:], d[:], sgnT[:, h:h + 1], None, ALU.mult)),
                [(vv.h, vv.n), gain_rdy], vv)
            out_holder.append((vv.h, vv.n))

        import os as _osA
        _gstep = int(_osA.environ.get("A3_GSTEP", "16"))

        def emit_attn(h, q_rdy, gen=None):
            kv = h // 4
            par = h % 2
            last_yev = None
            for qt in range(NT):
                kt0 = max(0, qt - 4)
                kts = list(range(kt0, qt + 1))
                nk = len(kts)
                cur = qt % 2
                pb = pT[qt % 2]
                for si, kt in enumerate(kts):
                    wt = []
                    if si == 0:
                        wt = [q_rdy, gain_rdy]
                        if st["exp_prev"] is not None:
                            wt.append(st["exp_prev"])
                    masked = (si == nk - 1) or (qt >= 4 and si == 0)
                    p.op("tensor", (lambda e, si=si, kt=kt, kv=kv, par=par,
                                    qt=qt, masked=masked:
                         e.matmul(sc_ps[:, si * 128:(si + 1) * 128],
                                  k_fm[kv][:, kt * 128:(kt + 1) * 128],
                                  q_fm[par][:, qt * 128:(qt + 1) * 128],
                                  start=True, stop=not masked)),
                         wt, None)
                    if masked:
                        mb = maskd if si == nk - 1 else maskf
                        p.op("tensor", (lambda e, si=si, mb=mb: e.matmul(
                            sc_ps[:, si * 128:(si + 1) * 128], identb[:],
                            mb[:], start=False, stop=True)), (), None)
                p.op("tensor", (lambda e: e.nop()), (), mm)
                ewt = [(mm.h, mm.n), gain_rdy_s]
                if st["av_prev"][qt % 2] is not None:
                    ewt.append(st["av_prev"][qt % 2])
                p.op("scalar", (lambda e, nk=nk, h=h, pb=pb: e.activation(
                    pb[:, 0:nk * 128], sc_ps[:, 0:nk * 128], AF.Exp,
                    scale=scaleTA[:, h:h + 1], bias=biasT[:, h:h + 1])),
                    ewt, ss)
                st["exp_prev"] = (ss.h, ss.n)
                exp_w = (ss.h, ss.n)
                for si in range(nk):
                    wt = []
                    if si == 0:
                        wt = [exp_w]
                        if st["yA_use"][cur] is not None:
                            wt.append(st["yA_use"][cur])
                    p.op("tensor", (lambda e, si=si, cur=cur, nk=nk, pb=pb:
                         e.matmul(yA[cur][0:1, 128:256], onescb[:],
                                  pb[:, si * 128:(si + 1) * 128],
                                  start=(si == 0), stop=(si == nk - 1))),
                         wt, None)
                for si, kt in enumerate(kts):
                    p.op("tensor", (lambda e, si=si, kt=kt, kv=kv, cur=cur,
                                    nk=nk, pb=pb:
                         e.matmul(yA[cur][:, 0:128],
                                  v_tm[kt][:, kv * 128:(kv + 1) * 128],
                                  pb[:, si * 128:(si + 1) * 128],
                                  start=(si == 0), stop=(si == nk - 1))),
                         (), mm if si == nk - 1 else None)
                st["av_prev"][qt % 2] = (mm.h, mm.n)
                wt = [(mm.h, mm.n)]
                if st["recip_use"][qt % 2] is not None:
                    wt.append(st["recip_use"][qt % 2])
                p.op("vector", (lambda e, cur=cur, q2=qt % 2: e.reciprocal(
                    recipb[q2][:], yA[cur][0:1, 128:256])), wt, vv)
                p.op("tensor", (lambda e, cur=cur, q2=qt % 2: e.matmul(
                    yA[cur][:, 256:384], onesrf[:], recipb[q2][:],
                    start=True, stop=True)), [(vv.h, vv.n)], mm2)
                st["recip_use"][qt % 2] = (mm2.h, mm2.n)
                wt = [(mm2.h, mm2.n)]
                if st["rB_use"][qt % 2] is not None:
                    wt.append(st["rB_use"][qt % 2])
                p.op("scalar", (lambda e, cur=cur, q2=qt % 2: e.copy(
                    rB_sb[q2][:], yA[cur][:, 256:384])), wt, ss)
                p.op("vector", (lambda e, cur=cur, q2=qt % 2, h=h, qt=qt:
                     e.tensor_mul(y_t[h][:, qt * 128:(qt + 1) * 128],
                                  yA[cur][:, 0:128], rB_sb[q2][:])),
                     [(ss.h, ss.n)], vv)
                st["rB_use"][qt % 2] = (vv.h, vv.n)
                st["yA_use"][cur] = (vv.h, vv.n)
                last_yev = (vv.h, vv.n)
                if gen is not None and qt % _gstep == _gstep - 1:
                    next(gen, None)
            st["qfm_use"][par] = last_yev

        import os
        nheads = int(os.environ.get("A3_HEADS", str(H)))
        do_attn = os.environ.get("A3_ATTN", "1") == "1"
        hold = []
        emit_qproj_mm(0)
        g0 = qproj_rms_gen(0, hold)
        for _ in g0:
            pass
        q_rdy = hold.pop()
        for h in range(nheads):
            if h + 1 < nheads:
                emit_qproj_mm(h + 1)
                gen = qproj_rms_gen(h + 1, hold)
            else:
                gen = None
            if os.environ.get("A3_NOILV", "0") == "1" and gen is not None:
                for _ in gen:
                    pass
                gen = None
                nq = hold.pop()
            if do_attn:
                emit_attn(h, q_rdy, gen)
            if gen is not None:
                for _ in gen:
                    pass
                q_rdy = hold.pop()
            elif h + 1 < nheads and os.environ.get("A3_NOILV", "0") == "1":
                q_rdy = nq
        esQp.close()
        esQ.close()
        y_rdy = (vv.h, vv.n)

        if "y" in dbg:
            d = dbg_out("y", [128, H * S], BF16)
            for h in range(H):
                p.dma("gpsimd", d[:, h * S:(h + 1) * S], y_t[h][:], [y_rdy], c_ld)

    if stage_n >= 5:
        p.barrier(bar)
        esA.close()
        esF0 = contextlib.ExitStack()
        f0["bufs"] = [sb_in(esF0, f"f0b{i}", [128, DM], F32) for i in range(4)]
        esO = contextlib.ExitStack()
        wot_b = [sb_in(esO, f"wotb{i}", [128, DM], BF16) for i in range(2)]
        a4tmp = [sb_in(esO, f"a4tmp{i}", [128, 512], F32) for i in range(2)]
        a4out = [sb_in(esO, f"a4out{i}", [128, 512], F32) for i in range(2)]
        esOp = contextlib.ExitStack()
        a4_ps = [ps_in(esOp, f"a4ps{i}", [128, 512]) for i in range(2)]
        a4t_ps = [ps_in(esOp, f"a4tps{i}", [128, 512]) for i in range(2)]

        wot_b_use = [None, None]
        a4ps_use = [None, None]
        a4tps_use = [None, None]
        a4tmp_use = [None, None]
        a4out_use = [None, None]
        barw = (bar.h, bar.n)
        a4_pend = []

        def _flush_a4(item):
            sc4, m, par, evac_w = item
            for qq in range(4):
                wt = []
                if qq == 0:
                    wt = [evac_w]
                    if a4tps_use[par] is not None:
                        wt.append(a4tps_use[par])
                p.op("tensor", (lambda e, qq=qq, par=par: e.matmul(
                    a4t_ps[par][:, qq * 128:(qq + 1) * 128],
                    a4tmp[par][:, qq * 128:(qq + 1) * 128],
                    identf[:], start=True, stop=True)),
                    wt, mm2 if qq == 3 else None)
            a4tmp_use[par] = (mm2.h, mm2.n)
            wt = [(mm2.h, mm2.n)]
            if a4out_use[par] is not None:
                wt.append(a4out_use[par])
            p.op("scalar", (lambda e, par=par: e.copy(a4out[par][:],
                                                      a4t_ps[par][:])), wt, ss)
            a4tps_use[par] = (ss.h, ss.n)
            p.dma("gpsimd",
                  bass.AP(x1_t, sc4 * 512 * DM + m * 128,
                          [[DM, 128], [128 * DM, 4], [1, 128]]),
                  a4out[par][:], [(ss.h, ss.n), x1p_rdy], d_acc[par],
                  accum_op=ALU.add)
            a4out_use[par] = (d_acc[par].h, d_acc[par].n)
        for sc4 in range(4):
            for m in range(ND):
                par = m % 2
                wt = list(wot_all) + ([wot_b_use[par]] if wot_b_use[par] else [])
                p.dma("sync", wot_b[par][:],
                      bass.AP(wot_t, m * DM * 128,
                              [[128, 128], [128 * 128, 16], [1, 128]]),
                      wt, d_wotl[par])
                ld = (d_wotl[par].h, d_wotl[par].n)
                for yc in range(H):
                    wt = []
                    if yc == 0:
                        wt = [ld, barw]
                        if a4ps_use[par] is not None:
                            wt.append(a4ps_use[par])
                    p.op("tensor", (lambda e, yc=yc, par=par, sc4=sc4: e.matmul(
                        a4_ps[par][:], wot_b[par][:, yc * 128:(yc + 1) * 128],
                        y_t[yc][:, sc4 * 512:(sc4 + 1) * 512],
                        start=(yc == 0), stop=(yc == H - 1))),
                        wt, mm if yc == H - 1 else None)
                wot_b_use[par] = (mm.h, mm.n)
                wt = [(mm.h, mm.n)]
                if a4tmp_use[par] is not None:
                    wt.append(a4tmp_use[par])
                p.op("scalar", (lambda e, par=par: e.copy(a4tmp[par][:],
                                                          a4_ps[par][:])), wt, ss)
                a4ps_use[par] = (ss.h, ss.n)
                a4_pend.append((sc4, m, par, (ss.h, ss.n)))
                if len(a4_pend) >= 2:
                    _flush_a4(a4_pend.pop(0))
                f0_steps(3)
        while a4_pend:
            _flush_a4(a4_pend.pop(0))
        acc_all = [(d_acc[0].h, d_acc[0].n), (d_acc[1].h, d_acc[1].n)]
        esOp.close()
        esO.close()

        esL = contextlib.ExitStack()
        x1b = [sb_in(esL, f"x1b{i}", [128, DM], F32) for i in range(2)]
        ln2b = [sb_in(esL, f"ln2b{i}", [128, DM], BF16) for i in range(2)]
        ln2tmp = [sb_in(esL, f"ln2tmp{i}", [128, DM], BF16) for i in range(2)]
        junk2 = sb_in(esL, "junk2", [128, 1024], F32)
        st2 = sb_in(esL, "st2", [128, 8], F32)
        esLp = contextlib.ExitStack()
        tps2 = [ps_in(esLp, f"tpl{i}", [128, 512]) for i in range(2)]
        tps_save = tps
        tps = tps2
        ln_norm = []
        ln_tp = []
        ln2tmp_use = [None, None]
        for t in range(NT):
            par = t % 2
            sc4, qq = t // 4, t % 4
            gate = list(acc_all) + ([ln_norm[t - 2]] if t >= 2 else [])
            p.dma("sync", x1b[par][:], x1_h[t * 128:(t + 1) * 128, :],
                  gate, d_x1l[par])
            ld = (d_x1l[par].h, d_x1l[par].n)
            norm_extra = [ln_tp[t - 2]] if t >= 2 else ()
            nd = emit_ln(x1b[par], ln2b[par], st2, junk2, [ld], norm_extra)
            ln_norm.append(nd)

            def cp2(e, g, tp, par=par):
                return e.tensor_copy(
                    ln2tmp[par].ap().rearrange("p (j c) -> p j c", j=ND)
                    [:, g * 4:g * 4 + 4, :],
                    tp.ap().rearrange("p (a b) -> p a b", a=4)[:, :, :])
            trw = [nd] + ([ln2tmp_use[par]] if ln2tmp_use[par] else [])
            tr = emit_transpose16(ln2b[par], cp2, trw, identb, mm2)
            ln_tp.append(tr)
            p.dma("gpsimd",
                  bass.AP(ln2t_t, sc4 * DM * 512 + qq * 128,
                          [[512, 128], [128 * 512, 16], [1, 128]]),
                  ln2tmp[par][:], [tr], d_ln2[par])
            ln2tmp_use[par] = (d_ln2[par].h, d_ln2[par].n)
            f0_steps(4)
        ln2_all = [(d_ln2[0].h, d_ln2[0].n), (d_ln2[1].h, d_ln2[1].n)]
        x1_done = ln_norm[-1]
        for i in range(4):
            p.dma("gpsimd", out_h[i * 512:(i + 1) * 512, :],
                  x1_h[i * 512:(i + 1) * 512, :], list(acc_all), d_outp)
        outp_all = (d_outp.h, d_outp.n)
        f0_steps(192)
        def tq_finalize2(w_i, acc_col, nw, want_scale_bcast, psum):
            p.op("tensor", (lambda e: e.matmul(
                psum[0:1, 0:1], tq_acc[:, acc_col:acc_col + 1], onescf[:],
                start=True, stop=True)),
                [(vself.h, vself.n), (sself.h, sself.n), (vv.h, vv.n),
                 (mm2.h, mm2.n), (ss.h, ss.n)], mm2)
            p.op("vector", (lambda e, w=w_i, nw=nw: e.tensor_scalar(
                tq_row[:, w:w + 1], psum[0:1, 0:1], 1.0 / nw, EPS_TQ,
                ALU.mult, ALU.add)), [(mm2.h, mm2.n)], vself)
            p.op("vector", (lambda e, w=w_i: e.reciprocal(
                tq_row[:, 4 + w:5 + w], tq_row[:, w:w + 1])),
                [(vself.h, vself.n)], vself)
            p.op("tensor", (lambda e, w=w_i: e.matmul(
                psum[:, 0:1], onesrf[:], tq_row[0:1, 4 + w:5 + w],
                start=True, stop=True)), [(vself.h, vself.n)], mm2)
            p.op("scalar", (lambda e, w=w_i: e.copy(tqt[:, w:w + 1],
                                                    psum[:, 0:1])),
                 [(mm2.h, mm2.n)], ss)
            if want_scale_bcast:
                p.op("tensor", (lambda e, w=w_i: e.matmul(
                    psum[:, 128:129], onesrf[:], tq_row[0:1, w:w + 1],
                    start=True, stop=True)), [(ss.h, ss.n)], mm2)
                p.op("scalar", (lambda e: e.copy(tqt[:, 3:4],
                                                 psum[:, 128:129])),
                     [(mm2.h, mm2.n)], ss)

        tq_finalize2(0, 0, NW_UG, True, tps2[0])
        tq_finalize2(1, 1, NW_UG, False, tps2[0])
        tq_finalize2(2, 2, NW_D, False, tps2[0])
        tq_hold["ug"] = (ss.h, ss.n)
        p.op("vector", (lambda e: e.tensor_mul(
            tqt[:, 4:5], tqt[:, 1:2], tqt[:, 2:3])),
            [(vself.h, vself.n), (ss.h, ss.n)], vself)
        p.op("vector", (lambda e: e.reciprocal(
            tqt[:, 4:5], tqt[:, 4:5])), [(vself.h, vself.n)], vself)
        p.op("scalar", (lambda e: e.copy(tq_row[:, 7:8], tqt[0:1, 4:5])),
             [(vself.h, vself.n)], ss)
        tq_hold["sgd"] = (ss.h, ss.n)
        esLp.close()
        esL.close()
        esF0.close()

        if "x1" in dbg:
            d = dbg_out("x1", [S, DM], F32)
            p.dma("gpsimd", d[:], x1_h[:], list(acc_all), c_ld)

    if stage_n >= 6:
        p.barrier(bar)
        esY.close()
        esF = contextlib.ExitStack()
        h_sb = sb_in(esF, "h_sb", [128, DE * 8], BF16)
        esFq = contextlib.ExitStack()
        fwraw = [sb_in(esFq, f"fwraw{i}", [128, 1024], F32) for i in range(4)]
        tern = [sb_in(esFq, f"tern{i}", [128, DM], BF16) for i in range(2)]
        def tq_finalize(w_i, acc_col, nw, want_scale_bcast, psum, extra_wait=()):
            p.op("tensor", (lambda e: e.matmul(
                psum[0:1, 0:1], tq_acc[:, acc_col:acc_col + 1], onescf[:],
                start=True, stop=True)),
                [(vself.h, vself.n), (ss.h, ss.n)] + list(extra_wait), mm2)
            p.op("vector", (lambda e, w=w_i, nw=nw: e.tensor_scalar(
                tq_row[:, w:w + 1], psum[0:1, 0:1], 1.0 / nw, EPS_TQ,
                ALU.mult, ALU.add)), [(mm2.h, mm2.n)], vself)
            p.op("vector", (lambda e, w=w_i: e.reciprocal(
                tq_row[:, 4 + w:5 + w], tq_row[:, w:w + 1])),
                [(vself.h, vself.n)], vself)
            p.op("tensor", (lambda e, w=w_i: e.matmul(
                psum[:, 0:1], onesrf[:], tq_row[0:1, 4 + w:5 + w],
                start=True, stop=True)), [(vself.h, vself.n)], mm2)
            p.op("scalar", (lambda e, w=w_i: e.copy(tqt[:, w:w + 1],
                                                    psum[:, 0:1])),
                 [(mm2.h, mm2.n)], ss)
            if want_scale_bcast:
                p.op("tensor", (lambda e, w=w_i: e.matmul(
                    psum[:, 128:129], onesrf[:], tq_row[0:1, w:w + 1],
                    start=True, stop=True)), [(ss.h, ss.n)], mm2)
                p.op("scalar", (lambda e: e.copy(tqt[:, 3:4], psum[:, 128:129])),
                     [(mm2.h, mm2.n)], ss)

        def emit_quant_piece(src_hbm_slice, inv_col, dst_cols, par, tb,
                             extra=()):
            wt = list(extra)
            if fst["fw_use"][par] is not None:
                wt.append(fst["fw_use"][par])
            p.dma("sync", fwraw[par][:], src_hbm_slice, wt, d_fw[par])
            ld = (d_fw[par].h, d_fw[par].n)
            wt = [ld, tq_hold["ug"]]
            if fst["tern_use"][tb] is not None and dst_cols[0] == 0:
                wt.append(fst["tern_use"][tb])
            p.op("vector", (lambda e, par=par, c=inv_col: e.tensor_scalar(
                fwraw[par][:], fwraw[par][:], tqt[:, c:c + 1], -1.0,
                ALU.mult, ALU.max)), wt, vv)
            p.op("vector", (lambda e, par=par: e.tensor_scalar(
                fwraw[par][:], fwraw[par][:], 1.0, RC, ALU.min, ALU.add)),
                (), vv)
            p.op("vector", (lambda e, par=par, d=dst_cols, tb=tb:
                 e.tensor_scalar(tern[tb][:, d[0]:d[1]], fwraw[par][:], RC,
                                 None, ALU.subtract)), (), vv)
            qd = (vv.h, vv.n)
            fst["fw_use"][par] = qd
            return qd

        fst = {"fw_use": [None] * 4, "tern_use": [None, None]}
        import os as _os
        _nhf = int(_os.environ.get("F_HALVES", "2"))
        _skipf2 = _os.environ.get("F_SKIP_F2", "0") == "1"
        for hf in range(_nhf):
            esF1 = contextlib.ExitStack()
            lnh = sb_in(esF1, f"lnh{hf}", [128, ND * 1024], BF16)
            nwb = 1
            wut_l = [sb_in(esF1, f"wutsb{hf}_{b}", [128, DM], BF16)
                     for b in range(nwb)]
            wgt_l = [sb_in(esF1, f"wgtsb{hf}_{b}", [128, DM], BF16)
                     for b in range(nwb)]
            wut_sb = wut_l[0]
            wgt_sb = wgt_l[0]
            wuse = [None] * nwb
            guse = [None] * nwb
            silu_sb = sb_in(esF1, f"silu{hf}", [128, 1024], BF16)
            esF1p = contextlib.ExitStack()
            up_ps = ps_in(esF1p, f"up_ps{hf}", [128, 1024])
            gate_ps = ps_in(esF1p, f"gate_ps{hf}", [128, 1024])
            tpf = [ps_in(esF1p, f"tpf{hf}_{i}", [128, 512]) for i in range(2)]
            tps = tpf
            _fin_guard = tq_hold["ug"]
            tq_rdy_ug = (tq_hold["ug"], tq_hold["ug"])
            f1 = {"wut_use": None, "wgt_use": None, "up_use": _fin_guard,
                  "gate_use": None, "silu_use": _fin_guard}
            for j in range(ND):
                for c2 in range(2):
                    sc4 = hf * 2 + c2
                    p.dma("sync", lnh[:, j * 1024 + c2 * 512:
                                      j * 1024 + (c2 + 1) * 512],
                          bass.AP(ln2t_t, sc4 * DM * 512 + j * 128 * 512,
                                  [[512, 128], [1, 512]]),
                          list(ln2_all) + [(bar.h, bar.n)], d_ln2l)
            lnh_rdy = (d_ln2l.h, d_ln2l.n)
            pend_hTT = None
            for ec in range(NE):
                if hf == 0:
                    for piece in range(2):
                        qd_u = emit_quant_piece(
                            wu_h[ec * 128:(ec + 1) * 128,
                                 piece * 1024:(piece + 1) * 1024],
                            0, (piece * 1024, (piece + 1) * 1024), piece, 0)

                    if pend_hTT is not None:
                        pec, pw = pend_hTT
                        p.op("vector", (lambda e, ec=pec, g2=gate_ps,
                                        s2=silu_sb: e.tensor_mul(
                            h_sb[:, ec * 1024:(ec + 1) * 1024], g2[:],
                            s2[:])), pw, vv)
                        f1["gate_use"] = (vv.h, vv.n)
                        f1["silu_use"] = (vv.h, vv.n)
                        pend_hTT = None

                    def cpu_(e, g, tp, d=wut_sb):
                        return e.tensor_copy(d[:, g * 512:(g + 1) * 512], tp[:])
                    gw = [qd_u, (d_tc[0].h, d_tc[0].n)] + (
                        [f1["wut_use"]] if f1["wut_use"] else [])
                    tr_u = emit_transpose16(tern[0], cpu_, gw, identb, mm2)
                    fst["tern_use"][0] = tr_u
                    p.dma("gpsimd",
                          bass.AP(wut_t, ec * 128 * DM, [[DM, 128], [1, DM]]),
                          wut_sb[:], [tr_u], d_tc[0])
                    for piece in range(2):
                        qd_g = emit_quant_piece(
                            wg_h[ec * 128:(ec + 1) * 128,
                                 piece * 1024:(piece + 1) * 1024],
                            1, (piece * 1024, (piece + 1) * 1024), 2 + piece, 1)

                    def cpg_(e, g, tp, d=wgt_sb):
                        return e.tensor_copy(d[:, g * 512:(g + 1) * 512], tp[:])
                    gw = [qd_g, (d_tc[1].h, d_tc[1].n)] + (
                        [f1["wgt_use"]] if f1["wgt_use"] else [])
                    tr_g = emit_transpose16(tern[1], cpg_, gw, identb, mm2)
                    fst["tern_use"][1] = tr_g
                    p.dma("gpsimd",
                          bass.AP(wgt_t, ec * 128 * DM, [[DM, 128], [1, DM]]),
                          wgt_sb[:], [tr_g], d_tc[1])
                    rdy_u, rdy_g = tr_u, tr_g
                else:
                    if pend_hTT is not None:
                        pec, pw = pend_hTT
                        p.op("vector", (lambda e, ec=pec, g2=gate_ps,
                                        s2=silu_sb: e.tensor_mul(
                            h_sb[:, ec * 1024:(ec + 1) * 1024], g2[:],
                            s2[:])), pw, vv)
                        f1["gate_use"] = (vv.h, vv.n)
                        f1["silu_use"] = (vv.h, vv.n)
                        pend_hTT = None
                    wb = ec % nwb
                    wut_sb = wut_l[wb]
                    wgt_sb = wgt_l[wb]
                    wt = [(d_tc[0].h, d_tc[0].n)]
                    if wuse[wb] is not None:
                        wt.append(wuse[wb])
                    p.dma("sync", wut_sb[:],
                          bass.AP(wut_t, ec * 128 * DM, [[DM, 128], [1, DM]]),
                          wt, d_tcl[wb])
                    rdy_u = (d_tcl[wb].h, d_tcl[wb].n)
                    wt = [(d_tc[1].h, d_tc[1].n)]
                    if guse[wb] is not None:
                        wt.append(guse[wb])
                    p.dma("sync", wgt_sb[:],
                          bass.AP(wgt_t, ec * 128 * DM, [[DM, 128], [1, DM]]),
                          wt, d_tcl[2 + wb])
                    rdy_g = (d_tcl[2 + wb].h, d_tcl[2 + wb].n)
                for (ps_t, w_sb, rdy, use_key) in (
                        (up_ps, wut_sb, rdy_u, "up_use"),
                        (gate_ps, wgt_sb, rdy_g, "gate_use")):
                    for j in range(ND):
                        for nn in range(2):
                            wt = []
                            if j == 0 and nn == 0:
                                wt = [rdy, lnh_rdy]
                                if f1[use_key] is not None:
                                    wt.append(f1[use_key])
                            p.op("tensor", (lambda e, j=j, nn=nn, ps_t=ps_t,
                                            w=w_sb, lnh=lnh: e.matmul(
                                ps_t[:, nn * 512:(nn + 1) * 512],
                                w[:, j * 128:(j + 1) * 128],
                                lnh[:, j * 1024 + nn * 512:
                                    j * 1024 + (nn + 1) * 512],
                                start=(j == 0), stop=(j == ND - 1))),
                                wt, mm if (j == ND - 1 and nn == 1) else None)
                    if ps_t is up_ps:
                        up_mm = (mm.h, mm.n)
                    else:
                        gate_mm = (mm.h, mm.n)
                f1["wut_use"] = up_mm
                f1["wgt_use"] = gate_mm
                wuse[ec % nwb] = up_mm
                guse[ec % nwb] = gate_mm
                wt = [up_mm]
                if f1["silu_use"] is not None:
                    wt.append(f1["silu_use"])
                p.op("scalar", (lambda e, silu_sb=silu_sb, u=up_ps:
                     e.activation(silu_sb[:], u[:], AF.Silu,
                                  scale=tqt[:, 3:4])), wt, ss)
                f1["up_use"] = (ss.h, ss.n)
                pend_hTT = (ec, [(ss.h, ss.n), gate_mm])
            if pend_hTT is not None:
                pec, pw = pend_hTT
                p.op("vector", (lambda e, ec=pec, g2=gate_ps, s2=silu_sb:
                     e.tensor_mul(h_sb[:, ec * 1024:(ec + 1) * 1024], g2[:],
                                  s2[:])), pw, vv)
                pend_hTT = None
            h_rdy = (vv.h, vv.n)
            sgd_rdy = tq_hold["sgd"]
            if "h2" in dbg and hf == 1:
                d4 = dbg_out("h2", [128, DE * 8], BF16)
                p.dma("gpsimd", d4[:], h_sb[:], [h_rdy], c_ld)
                d6 = dbg_out("wdt0", [128, DE], BF16)
                p.dma("gpsimd", d6[:], bass.AP(wdt_t, 0, [[DE, 128], [1, DE]]),
                      [h_rdy], c_ld)
                d7 = dbg_out("wdt15", [128, DE], BF16)
                p.dma("gpsimd", d7[:], bass.AP(wdt_t, 15 * 128 * DE,
                                               [[DE, 128], [1, DE]]),
                      [h_rdy], c_ld)
            if "tq" in dbg and hf == 0:
                d1 = dbg_out("tqt", [128, 8], F32)
                p.dma("gpsimd", d1[:], tqt[:], [sgd_rdy], c_ld)
                d2 = dbg_out("tqrow", [1, 8], F32)
                p.dma("gpsimd", d2[:], tq_row[:], [sgd_rdy], c_ld)
                d3 = dbg_out("wut0", [128, DM], BF16)
                p.dma("gpsimd", d3[:], bass.AP(wut_t, 0, [[DM, 128], [1, DM]]),
                      [(d_tc[0].h, d_tc[0].n)], c_ld)
            if "h" in dbg and hf == 0:
                d = dbg_tensors.get("h") or dbg_out("h", [128, DE * 8], BF16)
                p.dma("gpsimd", d[:], h_sb[:], [h_rdy], c_ld)
            esF1p.close()
            esF1.close()
            if _skipf2:
                continue

            esF2 = contextlib.ExitStack()
            nwd = 1 if hf == 0 else 2
            wdt_l = [sb_in(esF2, f"wdtsb{hf}_{i}", [128, DE], BF16)
                     for i in range(nwd)]
            wdt_sb = wdt_l[0]
            f2tmp = [sb_in(esF2, f"f2tmp{hf}_{i}", [128, 512], F32)
                     for i in range(2)]
            f2out = [sb_in(esF2, f"f2out{hf}_{i}", [128, 512], F32)
                     for i in range(2)]
            esF2p = contextlib.ExitStack()
            f2_ps = [ps_in(esF2p, f"f2ps{hf}_{i}", [128, 512]) for i in range(2)]
            f2t_ps = [ps_in(esF2p, f"f2tps{hf}_{i}", [128, 512])
                      for i in range(2)]
            tps_prev, tps = tps, f2t_ps
            f2 = {"wdt_use": [None] * nwd, "ps_use": [None, None],
                  "tps_use": [None, None], "tmp_use": [None, None],
                  "out_use": [None, None]}
            f2_pend = []

            def _flush_f2(item):
                gq, m, par, evac_w = item
                for qq in range(4):
                    wt = []
                    if qq == 0:
                        wt = [evac_w, (vv.h, vv.n)]
                        if f2["tps_use"][par] is not None:
                            wt.append(f2["tps_use"][par])
                    p.op("tensor", (lambda e, qq=qq, par=par, tp2=f2t_ps,
                                    tl=f2tmp: e.matmul(
                        tp2[par][:, qq * 128:(qq + 1) * 128],
                        tl[par][:, qq * 128:(qq + 1) * 128],
                        identf[:], start=True, stop=True)),
                        wt, mm2 if qq == 3 else None)
                f2["tmp_use"][par] = (mm2.h, mm2.n)
                wt = [(mm2.h, mm2.n)]
                if f2["out_use"][par] is not None:
                    wt.append(f2["out_use"][par])
                p.op("scalar", (lambda e, par=par, ol=f2out, tp2=f2t_ps:
                     e.copy(ol[par][:], tp2[par][:])), wt, ss)
                f2["tps_use"][par] = (ss.h, ss.n)
                p.dma("gpsimd",
                      bass.AP(out_h.tensor, gq * 512 * DM + m * 128,
                              [[DM, 128], [128 * DM, 4], [1, 128]]),
                      f2out[par][:], [(ss.h, ss.n), outp_all], d_fo[par],
                      accum_op=ALU.add)
                f2["out_use"][par] = (d_fo[par].h, d_fo[par].n)
            for m in range(ND):
                wd_b = m % nwd
                wdt_sb = wdt_l[wd_b]
                if hf == 0:
                    for piece in range(8):
                        tb = (piece // 2) % 2
                        qd = emit_quant_piece(
                            wd_h[m * 128:(m + 1) * 128,
                                 piece * 1024:(piece + 1) * 1024],
                            2, ((piece % 2) * 1024, (piece % 2) * 1024 + 1024),
                            piece % 4, tb, extra=[sgd_rdy, h_rdy])
                        if piece % 2 == 1:
                            pp = piece // 2

                            def cpd_(e, g, tp, pp=pp, w=wdt_sb):
                                return e.tensor_copy(
                                    w[:, pp * 2048 + g * 512:
                                      pp * 2048 + (g + 1) * 512], tp[:])
                            gw = [qd]
                            if pp == 0:
                                gw.append((d_tc[2].h, d_tc[2].n))
                                if m == 0:
                                    gw.append((d_tc[0].h, d_tc[0].n))
                                    gw.append((d_tc[1].h, d_tc[1].n))
                                if f2["wdt_use"][wd_b]:
                                    gw.append(f2["wdt_use"][wd_b])
                                for pr in range(2):
                                    if f2["tps_use"][pr]:
                                        gw.append(f2["tps_use"][pr])
                            tr = emit_transpose16(tern[tb], cpd_, gw,
                                                  identb, mm2)
                            fst["tern_use"][tb] = tr
                    wdt_rdy = (vv.h, vv.n)
                    p.dma("gpsimd",
                          bass.AP(wdt_t, m * 128 * DE, [[DE, 128], [1, DE]]),
                          wdt_sb[:], [wdt_rdy], d_tc[2])
                else:
                    wt = [(d_tc[2].h, d_tc[2].n), h_rdy]
                    if f2["wdt_use"][wd_b] is not None:
                        wt.append(f2["wdt_use"][wd_b])
                    p.dma("sync", wdt_sb[:],
                          bass.AP(wdt_t, m * 128 * DE, [[DE, 128], [1, DE]]),
                          wt, d_tcl[m % 2])
                    wdt_rdy = (d_tcl[m % 2].h, d_tcl[m % 2].n)
                for ss2 in range(2):
                    gq = hf * 2 + ss2
                    par = (2 * m + ss2) % 2
                    for ec in range(NE):
                        wt = []
                        if ec == 0:
                            wt = [wdt_rdy, h_rdy, sgd_rdy]
                            if f2["ps_use"][par] is not None:
                                wt.append(f2["ps_use"][par])
                        p.op("tensor", (lambda e, ec=ec, ss2=ss2, par=par,
                                        w=wdt_sb, f2_ps=f2_ps:
                             e.matmul(f2_ps[par][:],
                                      w[:, ec * 128:(ec + 1) * 128],
                                      h_sb[:, ec * 1024 + ss2 * 512:
                                           ec * 1024 + (ss2 + 1) * 512],
                                      start=(ec == 0), stop=(ec == NE - 1))),
                             wt, mm if ec == NE - 1 else None)
                    f2["wdt_use"][wd_b] = (mm.h, mm.n)
                    wt = [(mm.h, mm.n)]
                    if f2["tmp_use"][par] is not None:
                        wt.append(f2["tmp_use"][par])
                    p.op("scalar", (lambda e, par=par, tl=f2tmp, pl=f2_ps:
                         e.activation(tl[par][:], pl[par][:], AF.Identity,
                                      scale=tqt[:, 4:5])), wt, ss)
                    f2["ps_use"][par] = (ss.h, ss.n)
                    f2_pend.append((gq, m, par, (ss.h, ss.n)))
                    if len(f2_pend) >= 2:
                        _flush_f2(f2_pend.pop(0))
            while f2_pend:
                _flush_f2(f2_pend.pop(0))
            esF2p.close()
            esF2.close()
            tps = tps_prev
            if hf == 0:
                esFq.close()
                p.barrier(bar)
        esF.close()

    gw = [(c.h, c.n) for c in
          (d_acc[0], d_acc[1], d_outp, d_fo[0], d_fo[1], d_wot[0], d_wot[1],
           d_ln2[0], d_ln2[1], d_tc[0], d_tc[1], d_tc[2], d_x1p, d_sw, c_ld)]
    p.wait("gpsimd", gw)
    sw = [(c.h, c.n) for c in (d_sc[0], d_sc[1], d_sc[2], d_sc[3],
                               d_ln2l, d_fw[0], d_fw[1], d_fw[2], d_fw[3],
                               d_tcl[0], d_tcl[1], d_tcl[2], d_tcl[3],
                               d_wotl[0], d_wotl[1],
                               d_x1l[0], d_x1l[1])]
    p.wait("sync", sw)

    with nc.Block() as block:
        p.emit(block)
    if stage_n < 5:
        if stage_n < 4:
            esTP.close()
        esA.close()
    if stage_n < 6:
        esY.close()
    es.close()
    return nc, dbg_tensors


def _consts():
    half = HD // 2
    inv_freq = 10000.0 ** (-np.arange(0, half, dtype=np.float32) / half)
    freqs = np.arange(S, dtype=np.float32)[:, None] * inv_freq[None, :]
    cos = np.cos(freqs).astype(np.float32)
    sin = np.sin(freqs).astype(np.float32)
    cosf = np.concatenate([cos.T, cos.T], axis=0).astype(ml_dtypes.bfloat16)
    sinf = np.concatenate([sin.T, sin.T], axis=0).astype(ml_dtypes.bfloat16)
    kq = np.arange(128)
    maskd = np.where(kq[None, :] >= kq[:, None], 0.0,
                     -1e38).astype(ml_dtypes.bfloat16)
    maskf = np.where(kq[None, :] < kq[:, None], 0.0,
                     -1e38).astype(ml_dtypes.bfloat16)
    sign = np.ones((128, 1), np.float32)
    sign[64:] = -1.0
    return {
        "identb": np.eye(128, dtype=ml_dtypes.bfloat16),
        "identf": np.eye(128, dtype=np.float32),
        "cosf": cosf, "sinf": sinf, "maskd": maskd, "maskf": maskf,
        "onescb": np.ones((128, 1), ml_dtypes.bfloat16),
        "onescf": np.ones((128, 1), np.float32),
        "onesrf": np.ones((1, 128), np.float32),
        "sign": sign,
        "epsc": np.full((128, 1), EPS_RMS, np.float32),
    }


def _in_maps(L0_attn_x, L0_attn_Wq, L0_attn_Wk, L0_attn_Wv, L0_attn_Wo,
             L0_attn_q_gain, L0_ffn_W_up, L0_ffn_W_gate, L0_ffn_W_down):
    consts = _consts()
    in_maps = []
    for b in range(8):
        m = dict(consts)
        m["x"] = np.ascontiguousarray(L0_attn_x[b])
        m["wq"] = L0_attn_Wq
        m["wk"] = L0_attn_Wk
        m["wv"] = L0_attn_Wv
        m["wo"] = L0_attn_Wo
        m["gain"] = np.ascontiguousarray(L0_attn_q_gain.reshape(1, H))
        m["wu"] = L0_ffn_W_up
        m["wg"] = L0_ffn_W_gate
        m["wd"] = L0_ffn_W_down
        in_maps.append(m)
    return in_maps


def kernel(L0_attn_x, L0_attn_Wq, L0_attn_Wk, L0_attn_Wv, L0_attn_Wo,
           L0_attn_q_gain, L0_ffn_W_up, L0_ffn_W_gate, L0_ffn_W_down):
    nc, _ = build_nc()
    in_maps = _in_maps(L0_attn_x, L0_attn_Wq, L0_attn_Wk, L0_attn_Wv,
                       L0_attn_Wo, L0_attn_q_gain, L0_ffn_W_up,
                       L0_ffn_W_gate, L0_ffn_W_down)
    res = run_bass_kernel_spmd(nc, in_maps, core_ids=list(range(8)))
    return np.stack([res.results[b]["out"] for b in range(8)], axis=0)


# version 12 (final)
# speedup vs baseline: 1.0728x; 1.0728x over previous
import numpy as np
import ml_dtypes
import concourse.bass as bass
import concourse.mybir as mybir
from concourse.bass_utils import run_bass_kernel_spmd

F32 = mybir.dt.float32
BF16 = mybir.dt.bfloat16
AF = mybir.ActivationFunctionType
ALU = mybir.AluOpType
AX = mybir.AxisListType.X

S = 2048
DM = 2048
H = 16
KVH = 4
HD = 128
DE = 8192
NT = S // 128
ND = DM // 128
NE = DE // 128
EPS_LN = 1e-5
EPS_RMS = 1.1920928955078125e-07
EPS_TQ = 1e-8
RC = 12582912.0
SQRT_HD = float(np.sqrt(128.0))
NW_UG = float(DE * DM)
NW_D = float(DM * DE)

ENGINES = ("sync", "gpsimd", "tensor", "vector", "scalar")


class Ctr:
    __slots__ = ("h", "n")

    def __init__(self, h):
        self.h = h
        self.n = 0


class Prog:
    def __init__(self):
        self.ops = []

    def op(self, eng, fn, waits=(), inc=None, amt=1):
        incs = ()
        if inc is not None:
            inc.n += amt
            incs = ((inc.h, inc.n) if False else (inc.h, amt),)
        self.ops.append((eng, fn, tuple(waits), incs))

    def dma(self, eng, out, in_, waits=(), inc=None, **kw):
        kw = dict(kw)
        self.op(eng, (lambda e, o=out, i=in_, k=kw: e.dma_start(out=o, in_=i, **k)),
                waits, inc, 16)

    def wait(self, eng, waits):
        self.op(eng, None, waits)

    dma_registry = []

    def barrier(self, bar):
        for e in ENGINES:
            waits = [(c.h, c.n) for (en, c) in Prog.dma_registry if en == e]
            self.op(e, (lambda eng: eng.nop()), waits, bar, 1)
        tgt = bar.n
        for e in ENGINES:
            self.wait(e, [(bar.h, tgt)])

    def emit(self, block):
        for name in ENGINES:
            dec = getattr(block, name)

            def body(eng, _name=name):
                for (e, fn, waits, incs) in self.ops:
                    if e != _name:
                        continue
                    for (hh, v) in waits:
                        eng.wait_ge(hh, v)
                    if fn is None:
                        if incs:
                            inst = eng.nop()
                            for (hh, a) in incs:
                                inst.then_inc(hh, a)
                        continue
                    inst = fn(eng)
                    for (hh, a) in incs:
                        inst.then_inc(hh, a)
            dec(body)


def build_nc(stop_after="F2", dbg=()):
    import contextlib
    order = ["A1", "A2K", "A2V", "WO", "A3", "A4", "F1", "F2"]
    stage_n = order.index(stop_after)

    nc = bass.Bass("TRN2", target_bir_lowering=False)
    p = Prog()

    x_h = nc.dram_tensor("x", [S, DM], F32, kind="ExternalInput").ap()
    wq_h = nc.dram_tensor("wq", [DM, DM], F32, kind="ExternalInput").ap()
    wk_h = nc.dram_tensor("wk", [512, DM], F32, kind="ExternalInput").ap()
    wv_h = nc.dram_tensor("wv", [512, DM], F32, kind="ExternalInput").ap()
    wo_h = nc.dram_tensor("wo", [DM, DM], F32, kind="ExternalInput").ap()
    gain_h = nc.dram_tensor("gain", [1, H], F32, kind="ExternalInput").ap()
    wu_h = nc.dram_tensor("wu", [DE, DM], F32, kind="ExternalInput").ap()
    wg_h = nc.dram_tensor("wg", [DE, DM], F32, kind="ExternalInput").ap()
    wd_h = nc.dram_tensor("wd", [DM, DE], F32, kind="ExternalInput").ap()
    identb_h = nc.dram_tensor("identb", [128, 128], BF16, kind="ExternalInput").ap()
    identf_h = nc.dram_tensor("identf", [128, 128], F32, kind="ExternalInput").ap()
    cosf_h = nc.dram_tensor("cosf", [128, S], BF16, kind="ExternalInput").ap()
    sinf_h = nc.dram_tensor("sinf", [128, S], BF16, kind="ExternalInput").ap()
    maskd_h = nc.dram_tensor("maskd", [128, 128], BF16, kind="ExternalInput").ap()
    maskf_h = nc.dram_tensor("maskf", [128, 128], BF16, kind="ExternalInput").ap()
    onescb_h = nc.dram_tensor("onescb", [128, 1], BF16, kind="ExternalInput").ap()
    onescf_h = nc.dram_tensor("onescf", [128, 1], F32, kind="ExternalInput").ap()
    onesrf_h = nc.dram_tensor("onesrf", [1, 128], F32, kind="ExternalInput").ap()
    sign_h = nc.dram_tensor("sign", [128, 1], F32, kind="ExternalInput").ap()
    epsc_h = nc.dram_tensor("epsc", [128, 1], F32, kind="ExternalInput").ap()
    out_h = nc.dram_tensor("out", [S, DM], F32, kind="ExternalOutput").ap()
    x1_t = nc.dram_tensor("x1_s", [S, DM], F32)
    x1_h = x1_t.ap()
    ln2t_t = nc.dram_tensor("ln2t_s", [4, DM, 512], BF16)
    wot_t = nc.dram_tensor("wot_s", [ND, DM, 128], BF16)
    wut_t = nc.dram_tensor("wut_s", [NE, 128, DM], BF16)
    wgt_t = nc.dram_tensor("wgt_s", [NE, 128, DM], BF16)
    wdt_t = nc.dram_tensor("wdt_s", [ND, 128, DE], BF16)
    out_t = out_h.tensor if hasattr(out_h, "tensor") else None

    dbg_tensors = {}
    tq_hold = {}

    def dbg_out(name, shape, dtype):
        t = nc.dram_tensor("dbg_" + name, shape, dtype, kind="ExternalOutput").ap()
        dbg_tensors[name] = t
        return t

    es = contextlib.ExitStack()

    def sem(name):
        return Ctr(es.enter_context(nc.semaphore(name)))

    mm = sem("mm")
    mm2 = sem("mm2")
    vv = sem("vv")
    ss = sem("ss")
    vself = sem("vself")
    sself = sem("sself")
    bar = sem("bar")
    c_ld = sem("c_ld")
    d_x = [sem("d_x0"), sem("d_x1")]
    d_w = [sem("d_w0"), sem("d_w1")]
    d_sw = sem("d_sw")
    d_sc = [sem("d_sc0"), sem("d_sc1"), sem("d_sc2"), sem("d_sc3")]
    d_wot = [sem("d_wot0"), sem("d_wot1")]
    d_wotl = [sem("d_wotl0"), sem("d_wotl1")]
    d_x1p = sem("d_x1p")
    d_acc = [sem("d_acc0"), sem("d_acc1")]
    d_x1l = [sem("d_x1l0"), sem("d_x1l1")]
    d_ln2 = [sem("d_ln20"), sem("d_ln21")]
    d_ln2l = sem("d_ln2l")
    d_outp = sem("d_outp")
    d_fo = [sem("d_fo0"), sem("d_fo1")]
    d_fw = [sem(f"d_fw{i}") for i in range(4)]
    d_tc = [sem("d_tc0"), sem("d_tc1"), sem("d_tc2")]
    d_tcl = [sem("d_tcl0"), sem("d_tcl1")]

    Prog.dma_registry = (
        [("gpsimd", c) for c in (c_ld, d_x[0], d_x[1], d_w[0], d_w[1], d_sw,
                                 d_wot[0], d_wot[1], d_x1p, d_acc[0], d_acc[1],
                                 d_ln2[0], d_ln2[1], d_outp, d_fo[0], d_fo[1],
                                 d_tc[0], d_tc[1], d_tc[2])] +
        [("sync", c) for c in (d_sc[0], d_sc[1], d_sc[2], d_sc[3],
                               d_wotl[0], d_wotl[1],
                               d_x1l[0], d_x1l[1], d_ln2l, d_fw[0], d_fw[1],
                               d_fw[2], d_fw[3], d_tcl[0], d_tcl[1])])

    def sb_in(stack, name, shape, dtype):
        return stack.enter_context(nc.sbuf_tensor(name, shape, dtype))

    def ps_in(stack, name, shape):
        return stack.enter_context(nc.psum_tensor(name, shape, F32))

    identb = sb_in(es, "sb_identb", [128, 128], BF16)
    identf = sb_in(es, "sb_identf", [128, 128], F32)
    cosf = sb_in(es, "sb_cosf", [128, S], BF16)
    sinf = sb_in(es, "sb_sinf", [128, S], BF16)
    maskd = sb_in(es, "sb_maskd", [128, 128], BF16)
    maskf = sb_in(es, "sb_maskf", [128, 128], BF16)
    onescb = sb_in(es, "sb_onescb", [128, 1], BF16)
    onescf = sb_in(es, "sb_onescf", [128, 1], F32)
    onesrf = sb_in(es, "sb_onesrf", [1, 128], F32)
    signc = sb_in(es, "sb_signc", [128, 1], F32)
    epsc = sb_in(es, "sb_epsc", [128, 1], F32)
    gainr = sb_in(es, "sb_gainr", [1, H], F32)
    scaleT = sb_in(es, "scaleT", [128, H], F32)
    scaleTA = sb_in(es, "scaleTA", [128, H], F32)
    sgnT = sb_in(es, "sgnT", [128, H], F32)
    biasT = sb_in(es, "biasT", [128, H], F32)
    tqt = sb_in(es, "tqt", [128, 8], F32)
    tq_acc = sb_in(es, "tq_acc", [128, 8], F32)
    tq_row = sb_in(es, "tq_row", [1, 8], F32)

    for dst, src in ((identb, identb_h), (identf, identf_h), (cosf, cosf_h),
                     (sinf, sinf_h), (maskd, maskd_h), (maskf, maskf_h),
                     (onescb, onescb_h), (onescf, onescf_h), (onesrf, onesrf_h),
                     (signc, sign_h), (epsc, epsc_h), (gainr, gain_h)):
        p.dma("gpsimd", dst[:], src[:], inc=c_ld)
    consts_rdy = (c_ld.h, c_ld.n)
    for i in range(4):
        p.dma("gpsimd", x1_h[i * 512:(i + 1) * 512, :],
              x_h[i * 512:(i + 1) * 512, :], inc=d_x1p)
    x1p_rdy = (d_x1p.h, d_x1p.n)

    gb_ps_guard = contextlib.ExitStack()
    gb_ps = ps_in(gb_ps_guard, "gb_ps", [128, 16])
    p.op("tensor", (lambda e: e.matmul(gb_ps[:], onesrf[:], gainr[:],
                                       start=True, stop=True)), [consts_rdy], mm2)
    p.op("vector", (lambda e: e.tensor_scalar(scaleT[:], gb_ps[:], 1.0 / SQRT_HD,
                                              None, ALU.mult)), [(mm2.h, mm2.n)], vv)
    p.op("scalar", (lambda e: e.activation(biasT[:], gb_ps[:], AF.Abs)),
         [(mm2.h, mm2.n)], sself)
    p.op("scalar", (lambda e: e.mul(biasT[:], biasT[:], -SQRT_HD)),
         [(sself.h, sself.n)], ss)
    p.op("scalar", (lambda e: e.activation(scaleTA[:], scaleT[:], AF.Abs)),
         [(vv.h, vv.n)], ss)
    p.op("vector", (lambda e: e.tensor_scalar(sgnT[:], scaleT[:], 0.0, None,
                                              ALU.is_ge)), [(vv.h, vv.n)], vv)
    p.op("vector", (lambda e: e.tensor_scalar(sgnT[:], sgnT[:], 2.0, -1.0,
                                              ALU.mult, ALU.add)),
         [(vv.h, vv.n)], vv)
    gain_rdy = (vv.h, vv.n)
    gain_rdy_s = (ss.h, ss.n)
    gb_ps_guard.close()

    f0 = {"i": 0, "buf_use": [None] * 4, "col5_use": None, "bufs": None}

    def f0_steps(n):
        import os as _os2
        if f0["bufs"] is None:
            return
        for _ in range(n):
            i = f0["i"]
            if i >= 192:
                return
            f0["i"] = i + 1
            if i < 64:
                w, hbm = 0, wu_h[i * 128:(i + 1) * 128, :]
            elif i < 128:
                w, hbm = 1, wg_h[(i - 64) * 128:(i - 63) * 128, :]
            else:
                r, piece = divmod(i - 128, 4)
                w, hbm = 2, wd_h[r * 128:(r + 1) * 128,
                                 piece * 2048:(piece + 1) * 2048]
            par = i % 4
            buf = f0["bufs"][par]
            wt = [f0["buf_use"][par]] if f0["buf_use"][par] else []
            p.dma("sync", buf[:], hbm, wt, d_sc[par])
            ld = (d_sc[par].h, d_sc[par].n)
            first = i in (0, 64, 128)
            acc = tq_acc[:, w:w + 1] if first else tq_acc[:, 5:6]
            wt = [ld]
            if f0["col5_use"] is not None:
                wt.append(f0["col5_use"])
            p.op("scalar", (lambda e, b=buf, a=acc: e.activation(
                b[:], b[:], AF.Abs, accum_out=a)), wt, sself)
            f0["buf_use"][par] = (sself.h, sself.n)
            if not first:
                p.op("vector", (lambda e, w=w: e.tensor_add(
                    tq_acc[:, w:w + 1], tq_acc[:, w:w + 1], tq_acc[:, 5:6])),
                    [(sself.h, sself.n)] + (
                        [(vself.h, vself.n)] if f0["col5_use"] else []), vself)
                f0["col5_use"] = (vself.h, vself.n)
            else:
                f0["col5_use"] = None

    esY = contextlib.ExitStack()
    y_t = [sb_in(esY, f"y{i}", [128, S], BF16) for i in range(H)]

    esA = contextlib.ExitStack()
    lnT = sb_in(esA, "lnT", [128, NT * S], BF16)
    lnT3 = lnT.ap().rearrange("p (j s) -> p j s", j=ND)
    k_fm = [sb_in(esA, f"kfm{i}", [128, S], BF16) for i in range(KVH)]
    v_tm = [sb_in(esA, f"vtm{i}", [128, 512], BF16) for i in range(NT)]
    wraw = [sb_in(esA, "wraw0", [128, DM], BF16)]
    wraw_use = [None]

    esTP = contextlib.ExitStack()
    tps = [ps_in(esTP, f"tp{i}", [128, 512]) for i in range(2)]

    tp_guard = {}

    def emit_transpose16_gen(src_sb, dst_copy_fns, first_wait, ident, mmctr):
        for g in range(4):
            tp = tps[g % 2]
            wt = list(first_wait) if g == 0 else []
            gg = tp_guard.get(id(tp))
            if gg is not None:
                wt.append(gg)
            for jj in range(4):
                j = g * 4 + jj
                p.op("tensor", (lambda e, j=j, jj=jj, tp=tp, s=src_sb, idn=ident:
                     e.matmul(tp[:, jj * 128:(jj + 1) * 128],
                              s[:, j * 128:(j + 1) * 128], idn[:],
                              start=True, stop=True)),
                     wt if jj == 0 else (), mmctr if jj == 3 else None)
            p.op("vector", (lambda e, g=g, tp=tp, f=dst_copy_fns: f(e, g, tp)),
                 [(mmctr.h, mmctr.n)], vv)
            tp_guard[id(tp)] = (vv.h, vv.n)
            yield (vv.h, vv.n)

    def emit_transpose16(src_sb, dst_copy_fns, first_wait, ident, mmctr):
        tr = None
        for tr in emit_transpose16_gen(src_sb, dst_copy_fns, first_wait,
                                       ident, mmctr):
            pass
        return tr

    esA1 = contextlib.ExitStack()
    xbuf = [sb_in(esA1, f"xbuf{i}", [128, DM], F32) for i in range(2)]
    lnbuf = [sb_in(esA1, "lnbuf0", [128, DM], BF16)]
    junk1 = sb_in(esA1, "junk1", [128, 1024], F32)
    st1 = sb_in(esA1, "st1", [128, 8], F32)

    def emit_ln(xsrc, lnb, stats, junk, wait_x, norm_extra=()):
        p.op("vector", (lambda e, x=xsrc: e.reduce_sum(stats[:, 0:1], x[:], AX)),
             wait_x, vself)
        p.op("scalar", (lambda e, x=xsrc: e.activation(
            junk[:], x[:, 0:1024], AF.Square, accum_out=stats[:, 1:2])),
            wait_x, sself)
        p.op("scalar", (lambda e, x=xsrc: e.activation(
            junk[:], x[:, 1024:2048], AF.Square, accum_out=stats[:, 2:3])),
            (), sself)
        sq_done = (sself.h, sself.n)
        sum_done = (vself.h, vself.n)
        p.op("vector", (lambda e: e.tensor_scalar(stats[:, 3:4], stats[:, 0:1],
                                                  1.0 / DM, None, ALU.mult)),
             [sq_done, sum_done], vself)
        p.op("vector", (lambda e: e.tensor_add(stats[:, 4:5], stats[:, 1:2],
                                               stats[:, 2:3])), (), vself)
        p.op("vector", (lambda e: e.tensor_scalar(stats[:, 4:5], stats[:, 4:5],
                                                  1.0 / DM, EPS_LN, ALU.mult,
                                                  ALU.add)),
             [(vself.h, vself.n)], vself)
        p.op("vector", (lambda e: e.tensor_mul(stats[:, 7:8], stats[:, 3:4],
                                               stats[:, 3:4])),
             [(vself.h, vself.n)], vself)
        p.op("vector", (lambda e: e.tensor_sub(stats[:, 4:5], stats[:, 4:5],
                                               stats[:, 7:8])),
             [(vself.h, vself.n)], vself)
        p.op("scalar", (lambda e: e.activation(stats[:, 5:6], stats[:, 4:5],
                                               AF.Ln)), [(vself.h, vself.n)], sself)
        p.op("scalar", (lambda e: e.activation(stats[:, 5:6], stats[:, 5:6],
                                               AF.Exp, scale=-0.5)),
             [(sself.h, sself.n)], sself)
        p.op("scalar", (lambda e: e.activation(stats[:, 6:7], stats[:, 3:4],
                                               AF.Identity, scale=stats[:, 5:6])),
             [(sself.h, sself.n)], sself)
        p.op("scalar", (lambda e: e.mul(stats[:, 6:7], stats[:, 6:7], -1.0)),
             [(sself.h, sself.n)], sself)
        p.op("scalar", (lambda e, x=xsrc, o=lnb: e.activation(
            o[:], x[:], AF.Identity, bias=stats[:, 6:7], scale=stats[:, 5:6])),
            [(sself.h, sself.n)] + list(norm_extra), ss)
        return (ss.h, ss.n)

    wotmp = [sb_in(esA1, f"wotmp{i}", [128, DM], BF16) for i in range(2)]
    wo_state = {"m": 0, "use": [None, None], "written": []}

    def wo_step():
        m = wo_state["m"]
        if m >= ND:
            return
        wo_state["m"] = m + 1
        par = m % 2
        wt = [wraw_use[0]] if wraw_use[0] else [consts_rdy]
        p.dma("gpsimd", wraw[0][:], wo_h[m * 128:(m + 1) * 128, :], wt, d_w[0])
        ld = (d_w[0].h, d_w[0].n)

        def cpo(e, g, tp, w=wotmp[par]):
            return e.tensor_copy(w[:, g * 512:(g + 1) * 512], tp[:])
        trw = [ld] + ([wo_state["use"][par]] if wo_state["use"][par] else [])
        tr = emit_transpose16(wraw[0], cpo, trw, identb, mm2)
        wraw_use[0] = tr
        p.dma("gpsimd",
              bass.AP(wot_t, m * DM * 128, [[128, 128], [128 * 128, 16],
                                            [1, 128]]),
              wotmp[par][:], [tr], d_wot[par])
        wo_state["use"][par] = (d_wot[par].h, d_wot[par].n)
        wo_state["written"].append((d_wot[par].h, d_wot[par].n))

    a1_norm = []
    a1_tp = []
    for t in range(NT):
        par = t % 2
        gate = [a1_norm[t - 2]] if t >= 2 else [consts_rdy]
        p.dma("gpsimd", xbuf[par][:], x_h[t * 128:(t + 1) * 128, :], gate, d_x[par])
        ld = (d_x[par].h, d_x[par].n)
        norm_extra = [a1_tp[t - 1]] if t >= 1 else ()
        nd = emit_ln(xbuf[par], lnbuf[0], st1, junk1, [ld], norm_extra)
        a1_norm.append(nd)

        def cp(e, g, tp, t=t):
            return e.tensor_copy(lnT3[:, g * 4:g * 4 + 4, t * 128:(t + 1) * 128],
                          tp.ap().rearrange("p (a b) -> p a b", a=4)[:, :, :])
        fw = [nd, gain_rdy] if t == 0 else [nd]
        a1_tp.append(emit_transpose16(lnbuf[0], cp, fw, identb, mm2))
        wo_step()
    wot_all = [(d_wot[0].h, d_wot[0].n), (d_wot[1].h, d_wot[1].n)]
    esA1.close()
    lnT_rdy = (vv.h, vv.n)

    if "lnT" in dbg:
        d = dbg_out("lnT", [128, NT * S], BF16)
        p.dma("gpsimd", d[:], lnT[:], [lnT_rdy], c_ld)

    kv_rdy = None
    if stage_n >= 1:
        esK = contextlib.ExitStack()
        wT = sb_in(esK, "wT", [128, DM], BF16)
        khat = sb_in(esK, "khat", [128, S], BF16)
        khat_sw = sb_in(esK, "khat_sw", [128, S], BF16)
        sqb = [sb_in(esK, "sqb0", [128, 512], BF16)]
        rrow = [sb_in(esK, "rrow0", [1, 512], F32)]
        rstdB = [sb_in(esK, f"rstdB{i}", [128, 512], F32) for i in range(2)]
        esKp = contextlib.ExitStack()
        k_ps = ps_in(esKp, "k_ps", [128, S])
        sum_ps = ps_in(esKp, "sum_ps", [128, 512])
        bc_ps = ps_in(esKp, "bc_ps", [128, 512])

        def emit_rms(src_ps, n_cols, sq_pair, row_pair, rB_pair, mm_wait,
                     prev_use):
            nch = n_cols // 512
            sums_use = None
            bc_use = None
            row_use = [None]
            for ncI in range(nch):
                a, b = ncI * 512, (ncI + 1) * 512
                sq = sq_pair[0]
                row = row_pair[0]
                wt = [mm_wait]
                if sq_pair_use[0] is not None:
                    wt.append(sq_pair_use[0])
                p.op("scalar", (lambda e, a=a, b=b, sq=sq: e.activation(
                    sq[:], src_ps[:, a:b], AF.Square)), wt, ss)
                wt = [(ss.h, ss.n)]
                if sums_use is not None:
                    wt.append(sums_use)
                p.op("tensor", (lambda e, sq=sq: e.matmul(
                    sum_ps[0:1, :], onescb[:], sq[:], start=True, stop=True)),
                    wt, mm2)
                sq_pair_use[0] = (mm2.h, mm2.n)
                wt = [(mm2.h, mm2.n)]
                if row_use[0] is not None:
                    wt.append(row_use[0])
                p.op("scalar", (lambda e, row=row: e.activation(
                    row[:], sum_ps[0:1, :], AF.Ln, scale=1.0 / HD,
                    bias=epsc[0:1, :])), wt, sself)
                sums_use = (sself.h, sself.n)
                p.op("scalar", (lambda e, row=row: e.activation(
                    row[:], row[:], AF.Exp, scale=-0.5)),
                    [(sself.h, sself.n)], sself)
                wt = [(sself.h, sself.n)]
                if bc_use is not None:
                    wt.append(bc_use)
                p.op("tensor", (lambda e, row=row: e.matmul(
                    bc_ps[:], onesrf[:], row[:], start=True, stop=True)),
                    wt, mm2)
                row_use[0] = (mm2.h, mm2.n)
                p.op("scalar", (lambda e, rB=rB_pair[ncI % 2]: e.copy(
                    rB[:], bc_ps[:])), [(mm2.h, mm2.n)], ss)
                bc_use = (ss.h, ss.n)
                wt = [(ss.h, ss.n)]
                if prev_use is not None and ncI == 0:
                    wt.append(prev_use)
                p.op("vector", (lambda e, a=a, b=b, rB=rB_pair[ncI % 2]:
                     e.tensor_mul(khat[:, a:b], src_ps[:, a:b], rB[:])), wt, vv)
            return (vv.h, vv.n)

        def emit_rope(n_cols, dst, norm_done):
            p.dma("gpsimd", khat_sw[0:64, :n_cols], khat[64:128, :n_cols],
                  [norm_done], d_sw)
            p.dma("gpsimd", khat_sw[64:128, :n_cols], khat[0:64, :n_cols],
                  (), d_sw)
            swd = (d_sw.h, d_sw.n)
            p.op("vector", (lambda e, n=n_cols: e.tensor_mul(
                khat[:, :n], khat[:, :n], cosf[:, :n])), [swd], vv)
            p.op("vector", (lambda e, n=n_cols: e.tensor_mul(
                khat_sw[:, :n], khat_sw[:, :n], sinf[:, :n])), (), vv)
            p.op("vector", (lambda e, n=n_cols, d=dst: e.scalar_tensor_tensor(
                d[:, :n], khat_sw[:, :n], signc[:], khat[:, :n],
                ALU.mult, ALU.add)), [(vv.h, vv.n)], vv)
            return (vv.h, vv.n)

        sq_pair_use = [None]
        k_ps_use = None
        for c in range(KVH):
            par = 0
            wt = [wraw_use[par]] if wraw_use[par] else [lnT_rdy]
            p.dma("gpsimd", wraw[par][:], wk_h[c * 128:(c + 1) * 128, :],
                  wt, d_w[par])
            ld = (d_w[par].h, d_w[par].n)

            def cpw(e, g, tp):
                return e.tensor_copy(wT[:, g * 512:(g + 1) * 512], tp[:])
            tr = emit_transpose16(wraw[par], cpw, [ld], identb, mm2)
            wraw_use[par] = tr
            for j in range(ND):
                for ncI in range(4):
                    wt = []
                    if j == 0 and ncI == 0:
                        wt = [tr]
                        if k_ps_use is not None:
                            wt.append(k_ps_use)
                    p.op("tensor", (lambda e, j=j, ncI=ncI: e.matmul(
                        k_ps[:, ncI * 512:(ncI + 1) * 512],
                        wT[:, j * 128:(j + 1) * 128],
                        lnT3[:, j, ncI * 512:(ncI + 1) * 512],
                        start=(j == 0), stop=(j == ND - 1))),
                        wt, mm if (j == ND - 1 and ncI == 3) else None)
            nd = emit_rms(k_ps, S, sqb, rrow, rstdB, (mm.h, mm.n), None)
            k_ps_use = nd
            emit_rope(S, k_fm[c], nd)
        esKp.close()
        esK.close()
        kv_rdy = (vv.h, vv.n)

        if "kfm" in dbg:
            d = dbg_out("kfm", [128, KVH * S], BF16)
            for c in range(KVH):
                p.dma("gpsimd", d[:, c * S:(c + 1) * S], k_fm[c][:], [kv_rdy], c_ld)

    if stage_n >= 2:
        esV = contextlib.ExitStack()
        wvt = sb_in(esV, "wvt", [128, DE], BF16)
        wvt3 = wvt.ap().rearrange("p (j c) -> p j c", j=ND)
        esVp = contextlib.ExitStack()
        v_ps = [ps_in(esVp, f"v_ps{i}", [128, 512]) for i in range(2)]
        for c in range(KVH):
            par = 0
            wt = [wraw_use[par]] if wraw_use[par] else []
            p.dma("gpsimd", wraw[par][:], wv_h[c * 128:(c + 1) * 128, :],
                  wt, d_w[par])
            ld = (d_w[par].h, d_w[par].n)

            def cpv(e, g, tp, c=c):
                return e.tensor_copy(wvt3[:, g * 4:g * 4 + 4, c * 128:(c + 1) * 128],
                              tp.ap().rearrange("p (a b) -> p a b", a=4)[:, :, :])
            wraw_use[par] = emit_transpose16(wraw[par], cpv, [ld], identb, mm2)
        wvt_rdy = (vv.h, vv.n)
        v_done = []
        for t in range(NT):
            for j in range(ND):
                wt = []
                if j == 0:
                    wt = [wvt_rdy]
                    if t >= 2:
                        wt.append(v_done[t - 2])
                p.op("tensor", (lambda e, t=t, j=j: e.matmul(
                    v_ps[t % 2][:], lnT3[:, j, t * 128:(t + 1) * 128],
                    wvt3[:, j, :], start=(j == 0), stop=(j == ND - 1))),
                    wt, mm if j == ND - 1 else None)
            p.op("scalar", (lambda e, t=t: e.copy(v_tm[t][:], v_ps[t % 2][:])),
                 [(mm.h, mm.n)], ss)
            v_done.append((ss.h, ss.n))
        esVp.close()
        esV.close()

        if "vtm" in dbg:
            d = dbg_out("vtm", [128, NT * 512], BF16)
            for t in range(NT):
                p.dma("gpsimd", d[:, t * 512:(t + 1) * 512], v_tm[t][:],
                      [(ss.h, ss.n)], c_ld)

    if stage_n >= 3:
        pass

    if stage_n >= 4:
        esTP.close()
        esQ = contextlib.ExitStack()
        wTq = sb_in(esQ, "wTq", [128, DM], BF16)
        qhat = sb_in(esQ, "qhat", [128, S], BF16)
        qhat_sw = sb_in(esQ, "qhat_sw", [128, S], BF16)
        qsqb = [sb_in(esQ, "qsqb0", [128, 512], BF16)]
        qrow = [sb_in(esQ, "qrow0", [1, 512], F32)]
        qrB = [sb_in(esQ, "qrB0", [128, 512], F32)]
        q_fm = [sb_in(esQ, f"qfm{i}", [128, S], BF16) for i in range(2)]
        pT = [sb_in(esQ, f"pT{i}", [128, 640], BF16) for i in range(2)]
        recipb = [sb_in(esQ, f"recipb{i}", [1, 128], F32) for i in range(2)]
        rB_sb = [sb_in(esQ, f"rBsb{i}", [128, 128], F32) for i in range(2)]
        esQp = contextlib.ExitStack()
        q_ps = ps_in(esQp, "q_ps", [128, 1024])
        qsum_ps = ps_in(esQp, "qsum_ps", [128, 512])
        qbc_ps = ps_in(esQp, "qbc_ps", [128, 512])
        sc_ps = ps_in(esQp, "sc_ps", [128, 1024])
        yA = [ps_in(esQp, f"yA{i}", [128, 512]) for i in range(2)]

        st = {
            "qsq_use": [None],
            "qsum_use": None, "qbc_use": None,
            "q_ps_use": None, "wTq_use": None,
            "exp_prev": None, "av_prev": [None, None],
            "yA_use": [None, None],
            "recip_use": [None, None], "rB_use": [None, None],
            "qfm_use": [None, None],
        }

        def emit_qproj_mm(h):
            par = h % 2
            wpar = 0
            wt = [wraw_use[wpar]] if wraw_use[wpar] else []
            p.dma("gpsimd", wraw[wpar][:], wq_h[h * 128:(h + 1) * 128, :],
                  wt, d_w[wpar])
            ld = (d_w[wpar].h, d_w[wpar].n)
            for g in range(4):
                wt = [ld]
                if g == 0 and st["q_ps_use"] is not None:
                    wt.append(st["q_ps_use"])
                if g >= 1:
                    wt = [(vv.h, vv.n)]
                for jj in range(4):
                    j = g * 4 + jj
                    p.op("tensor", (lambda e, j=j, jj=jj: e.matmul(
                        q_ps[:, jj * 128:(jj + 1) * 128],
                        wraw[0][:, j * 128:(j + 1) * 128], identb[:],
                        start=True, stop=True)),
                        wt if jj == 0 else (), mm2 if jj == 3 else None)
                p.op("vector", (lambda e, g=g: e.tensor_copy(
                    wTq[:, g * 512:(g + 1) * 512], q_ps[:, 0:512])),
                    [(mm2.h, mm2.n)], vv)
            tr = (vv.h, vv.n)
            wraw_use[wpar] = tr
            for j in range(ND):
                for nn in range(2):
                    wt = [tr] if (j == 0 and nn == 0) else []
                    p.op("tensor", (lambda e, j=j, nn=nn: e.matmul(
                        q_ps[:, nn * 512:(nn + 1) * 512],
                        wTq[:, j * 128:(j + 1) * 128],
                        lnT3[:, j, nn * 512:(nn + 1) * 512],
                        start=(j == 0), stop=(j == ND - 1))),
                        wt, mm if j == ND - 1 else None)

        def qproj_rms_gen(h, out_holder):
            par = h % 2
            for half in range(2):
                if half == 1:
                    for j in range(ND):
                        for nn in range(2):
                            wt = [(vv.h, vv.n)] if (j == 0 and nn == 0) else []
                            p.op("tensor", (lambda e, j=j, nn=nn: e.matmul(
                                q_ps[:, nn * 512:(nn + 1) * 512],
                                wTq[:, j * 128:(j + 1) * 128],
                                lnT3[:, j, 1024 + nn * 512:
                                     1024 + (nn + 1) * 512],
                                start=(j == 0), stop=(j == ND - 1))),
                                wt, mm if j == ND - 1 else None)
                    yield
                mmw = (mm.h, mm.n)
                for nn in range(2):
                    base = half * 1024 + nn * 512
                    sq = qsqb[0]
                    wt = [mmw]
                    if st["qsq_use"][0] is not None:
                        wt.append(st["qsq_use"][0])
                    p.op("scalar", (lambda e, nn=nn, sq=sq: e.activation(
                        sq[:], q_ps[:, nn * 512:(nn + 1) * 512], AF.Square)),
                        wt, ss)
                    wt = [(ss.h, ss.n)]
                    if st["qsum_use"] is not None:
                        wt.append(st["qsum_use"])
                    p.op("tensor", (lambda e, sq=sq: e.matmul(
                        qsum_ps[0:1, :], onescb[:], sq[:],
                        start=True, stop=True)), wt, mm2)
                    st["qsq_use"][0] = (mm2.h, mm2.n)
                    yield
                    qr = qrow[0]
                    wt = [(mm2.h, mm2.n)]
                    if st.get("qrow_use", [None])[0] is not None:
                        wt.append(st["qrow_use"][0])
                    p.op("scalar", (lambda e, qr=qr: e.activation(
                        qr[:], qsum_ps[0:1, :], AF.Ln, scale=1.0 / HD,
                        bias=epsc[0:1, :])), wt, sself)
                    st["qsum_use"] = (sself.h, sself.n)
                    p.op("scalar", (lambda e, qr=qr: e.activation(
                        qr[:], qr[:], AF.Exp, scale=-0.5)),
                        [(sself.h, sself.n)], sself)
                    wt = [(sself.h, sself.n)]
                    if st["qbc_use"] is not None:
                        wt.append(st["qbc_use"])
                    p.op("tensor", (lambda e, qr=qr: e.matmul(
                        qbc_ps[:], onesrf[:], qr[:],
                        start=True, stop=True)), wt, mm2)
                    st.setdefault("qrow_use", [None])[0] = (mm2.h, mm2.n)
                    yield
                    wt2 = [(mm2.h, mm2.n)]
                    if st.get("qrB_use") is not None:
                        wt2.append(st["qrB_use"])
                    p.op("scalar", (lambda e: e.copy(qrB[0][:], qbc_ps[:])),
                         wt2, ss)
                    st["qbc_use"] = (ss.h, ss.n)
                    p.op("vector", (lambda e, b=base, nn=nn: e.tensor_mul(
                        qhat[:, b:b + 512], q_ps[:, nn * 512:(nn + 1) * 512],
                        qrB[0][:])), [(ss.h, ss.n)], vv)
                    st["qrB_use"] = (vv.h, vv.n)
                    yield
            nd = (vv.h, vv.n)
            st["q_ps_use"] = nd
            wt = [nd]
            if st["qfm_use"][par] is not None:
                wt.append(st["qfm_use"][par])
            p.dma("gpsimd", qhat_sw[0:64, :], qhat[64:128, :], wt, d_sw)
            p.dma("gpsimd", qhat_sw[64:128, :], qhat[0:64, :], (), d_sw)
            swd = (d_sw.h, d_sw.n)
            p.op("vector", (lambda e: e.tensor_mul(qhat[:], qhat[:], cosf[:])),
                 [swd], vv)
            p.op("vector", (lambda e: e.tensor_mul(qhat_sw[:], qhat_sw[:],
                                                   sinf[:])), (), vv)
            p.op("vector", (lambda e, d=q_fm[par]: e.scalar_tensor_tensor(
                d[:], qhat_sw[:], signc[:], qhat[:], ALU.mult, ALU.add)),
                [(vv.h, vv.n)], vv)
            p.op("vector", (lambda e, d=q_fm[par], h=h: e.tensor_scalar(
                d[:], d[:], sgnT[:, h:h + 1], None, ALU.mult)),
                [(vv.h, vv.n), gain_rdy], vv)
            out_holder.append((vv.h, vv.n))

        import os as _osA
        _gstep = int(_osA.environ.get("A3_GSTEP", "16"))

        def emit_attn(h, q_rdy, gen=None):
            kv = h // 4
            par = h % 2
            last_yev = None
            for qt in range(NT):
                kt0 = max(0, qt - 4)
                kts = list(range(kt0, qt + 1))
                nk = len(kts)
                cur = qt % 2
                pb = pT[qt % 2]
                for si, kt in enumerate(kts):
                    wt = []
                    if si == 0:
                        wt = [q_rdy, gain_rdy]
                        if st["exp_prev"] is not None:
                            wt.append(st["exp_prev"])
                    masked = (si == nk - 1) or (qt >= 4 and si == 0)
                    p.op("tensor", (lambda e, si=si, kt=kt, kv=kv, par=par,
                                    qt=qt, masked=masked:
                         e.matmul(sc_ps[:, si * 128:(si + 1) * 128],
                                  k_fm[kv][:, kt * 128:(kt + 1) * 128],
                                  q_fm[par][:, qt * 128:(qt + 1) * 128],
                                  start=True, stop=not masked)),
                         wt, None)
                    if masked:
                        mb = maskd if si == nk - 1 else maskf
                        p.op("tensor", (lambda e, si=si, mb=mb: e.matmul(
                            sc_ps[:, si * 128:(si + 1) * 128], identb[:],
                            mb[:], start=False, stop=True)), (), None)
                p.op("tensor", (lambda e: e.nop()), (), mm)
                ewt = [(mm.h, mm.n), gain_rdy_s]
                if st["av_prev"][qt % 2] is not None:
                    ewt.append(st["av_prev"][qt % 2])
                p.op("scalar", (lambda e, nk=nk, h=h, pb=pb: e.activation(
                    pb[:, 0:nk * 128], sc_ps[:, 0:nk * 128], AF.Exp,
                    scale=scaleTA[:, h:h + 1], bias=biasT[:, h:h + 1])),
                    ewt, ss)
                st["exp_prev"] = (ss.h, ss.n)
                exp_w = (ss.h, ss.n)
                for si in range(nk):
                    wt = []
                    if si == 0:
                        wt = [exp_w]
                        if st["yA_use"][cur] is not None:
                            wt.append(st["yA_use"][cur])
                    p.op("tensor", (lambda e, si=si, cur=cur, nk=nk, pb=pb:
                         e.matmul(yA[cur][0:1, 128:256], onescb[:],
                                  pb[:, si * 128:(si + 1) * 128],
                                  start=(si == 0), stop=(si == nk - 1))),
                         wt, None)
                for si, kt in enumerate(kts):
                    p.op("tensor", (lambda e, si=si, kt=kt, kv=kv, cur=cur,
                                    nk=nk, pb=pb:
                         e.matmul(yA[cur][:, 0:128],
                                  v_tm[kt][:, kv * 128:(kv + 1) * 128],
                                  pb[:, si * 128:(si + 1) * 128],
                                  start=(si == 0), stop=(si == nk - 1))),
                         (), mm if si == nk - 1 else None)
                st["av_prev"][qt % 2] = (mm.h, mm.n)
                wt = [(mm.h, mm.n)]
                if st["recip_use"][qt % 2] is not None:
                    wt.append(st["recip_use"][qt % 2])
                p.op("vector", (lambda e, cur=cur, q2=qt % 2: e.reciprocal(
                    recipb[q2][:], yA[cur][0:1, 128:256])), wt, vv)
                p.op("tensor", (lambda e, cur=cur, q2=qt % 2: e.matmul(
                    yA[cur][:, 256:384], onesrf[:], recipb[q2][:],
                    start=True, stop=True)), [(vv.h, vv.n)], mm2)
                st["recip_use"][qt % 2] = (mm2.h, mm2.n)
                wt = [(mm2.h, mm2.n)]
                if st["rB_use"][qt % 2] is not None:
                    wt.append(st["rB_use"][qt % 2])
                p.op("scalar", (lambda e, cur=cur, q2=qt % 2: e.copy(
                    rB_sb[q2][:], yA[cur][:, 256:384])), wt, ss)
                p.op("vector", (lambda e, cur=cur, q2=qt % 2, h=h, qt=qt:
                     e.tensor_mul(y_t[h][:, qt * 128:(qt + 1) * 128],
                                  yA[cur][:, 0:128], rB_sb[q2][:])),
                     [(ss.h, ss.n)], vv)
                st["rB_use"][qt % 2] = (vv.h, vv.n)
                st["yA_use"][cur] = (vv.h, vv.n)
                last_yev = (vv.h, vv.n)
                if gen is not None and qt % _gstep == _gstep - 1:
                    next(gen, None)
            st["qfm_use"][par] = last_yev

        import os
        nheads = int(os.environ.get("A3_HEADS", str(H)))
        do_attn = os.environ.get("A3_ATTN", "1") == "1"
        hold = []
        emit_qproj_mm(0)
        g0 = qproj_rms_gen(0, hold)
        for _ in g0:
            pass
        q_rdy = hold.pop()
        for h in range(nheads):
            if h + 1 < nheads:
                emit_qproj_mm(h + 1)
                gen = qproj_rms_gen(h + 1, hold)
            else:
                gen = None
            if os.environ.get("A3_NOILV", "0") == "1" and gen is not None:
                for _ in gen:
                    pass
                gen = None
                nq = hold.pop()
            if do_attn:
                emit_attn(h, q_rdy, gen)
            if gen is not None:
                for _ in gen:
                    pass
                q_rdy = hold.pop()
            elif h + 1 < nheads and os.environ.get("A3_NOILV", "0") == "1":
                q_rdy = nq
        esQp.close()
        esQ.close()
        y_rdy = (vv.h, vv.n)

        if "y" in dbg:
            d = dbg_out("y", [128, H * S], BF16)
            for h in range(H):
                p.dma("gpsimd", d[:, h * S:(h + 1) * S], y_t[h][:], [y_rdy], c_ld)

    if stage_n >= 5:
        p.barrier(bar)
        esA.close()
        esF0 = contextlib.ExitStack()
        f0["bufs"] = [sb_in(esF0, f"f0b{i}", [128, DM], F32) for i in range(4)]
        esO = contextlib.ExitStack()
        wot_b = [sb_in(esO, f"wotb{i}", [128, DM], BF16) for i in range(2)]
        a4tmp = [sb_in(esO, f"a4tmp{i}", [128, 512], F32) for i in range(2)]
        a4out = [sb_in(esO, f"a4out{i}", [128, 512], F32) for i in range(2)]
        esOp = contextlib.ExitStack()
        a4_ps = [ps_in(esOp, f"a4ps{i}", [128, 512]) for i in range(2)]
        a4t_ps = [ps_in(esOp, f"a4tps{i}", [128, 512]) for i in range(2)]

        wot_b_use = [None, None]
        a4ps_use = [None, None]
        a4tps_use = [None, None]
        a4tmp_use = [None, None]
        a4out_use = [None, None]
        barw = (bar.h, bar.n)
        a4_pend = []

        def _flush_a4(item):
            sc4, m, par, evac_w = item
            for qq in range(4):
                wt = []
                if qq == 0:
                    wt = [evac_w]
                    if a4tps_use[par] is not None:
                        wt.append(a4tps_use[par])
                p.op("tensor", (lambda e, qq=qq, par=par: e.matmul(
                    a4t_ps[par][:, qq * 128:(qq + 1) * 128],
                    a4tmp[par][:, qq * 128:(qq + 1) * 128],
                    identf[:], start=True, stop=True)),
                    wt, mm2 if qq == 3 else None)
            a4tmp_use[par] = (mm2.h, mm2.n)
            wt = [(mm2.h, mm2.n)]
            if a4out_use[par] is not None:
                wt.append(a4out_use[par])
            p.op("scalar", (lambda e, par=par: e.copy(a4out[par][:],
                                                      a4t_ps[par][:])), wt, ss)
            a4tps_use[par] = (ss.h, ss.n)
            p.dma("gpsimd",
                  bass.AP(x1_t, sc4 * 512 * DM + m * 128,
                          [[DM, 128], [128 * DM, 4], [1, 128]]),
                  a4out[par][:], [(ss.h, ss.n), x1p_rdy], d_acc[par],
                  accum_op=ALU.add)
            a4out_use[par] = (d_acc[par].h, d_acc[par].n)
        for sc4 in range(4):
            for m in range(ND):
                par = m % 2
                wt = list(wot_all) + ([wot_b_use[par]] if wot_b_use[par] else [])
                p.dma("sync", wot_b[par][:],
                      bass.AP(wot_t, m * DM * 128,
                              [[128, 128], [128 * 128, 16], [1, 128]]),
                      wt, d_wotl[par])
                ld = (d_wotl[par].h, d_wotl[par].n)
                for yc in range(H):
                    wt = []
                    if yc == 0:
                        wt = [ld, barw]
                        if a4ps_use[par] is not None:
                            wt.append(a4ps_use[par])
                    p.op("tensor", (lambda e, yc=yc, par=par, sc4=sc4: e.matmul(
                        a4_ps[par][:], wot_b[par][:, yc * 128:(yc + 1) * 128],
                        y_t[yc][:, sc4 * 512:(sc4 + 1) * 512],
                        start=(yc == 0), stop=(yc == H - 1))),
                        wt, mm if yc == H - 1 else None)
                wot_b_use[par] = (mm.h, mm.n)
                wt = [(mm.h, mm.n)]
                if a4tmp_use[par] is not None:
                    wt.append(a4tmp_use[par])
                p.op("scalar", (lambda e, par=par: e.copy(a4tmp[par][:],
                                                          a4_ps[par][:])), wt, ss)
                a4ps_use[par] = (ss.h, ss.n)
                a4_pend.append((sc4, m, par, (ss.h, ss.n)))
                if len(a4_pend) >= 2:
                    _flush_a4(a4_pend.pop(0))
                f0_steps(3)
        while a4_pend:
            _flush_a4(a4_pend.pop(0))
        acc_all = [(d_acc[0].h, d_acc[0].n), (d_acc[1].h, d_acc[1].n)]
        esOp.close()
        esO.close()

        esL = contextlib.ExitStack()
        x1b = [sb_in(esL, f"x1b{i}", [128, DM], F32) for i in range(2)]
        ln2b = [sb_in(esL, f"ln2b{i}", [128, DM], BF16) for i in range(2)]
        ln2tmp = [sb_in(esL, f"ln2tmp{i}", [128, DM], BF16) for i in range(2)]
        junk2 = sb_in(esL, "junk2", [128, 1024], F32)
        st2 = sb_in(esL, "st2", [128, 8], F32)
        esLp = contextlib.ExitStack()
        tps2 = [ps_in(esLp, f"tpl{i}", [128, 512]) for i in range(2)]
        tps_save = tps
        tps = tps2
        ln_norm = []
        ln_tp = []
        ln2tmp_use = [None, None]
        for t in range(NT):
            par = t % 2
            sc4, qq = t // 4, t % 4
            gate = list(acc_all) + ([ln_norm[t - 2]] if t >= 2 else [])
            p.dma("sync", x1b[par][:], x1_h[t * 128:(t + 1) * 128, :],
                  gate, d_x1l[par])
            ld = (d_x1l[par].h, d_x1l[par].n)
            norm_extra = [ln_tp[t - 2]] if t >= 2 else ()
            nd = emit_ln(x1b[par], ln2b[par], st2, junk2, [ld], norm_extra)
            ln_norm.append(nd)

            def cp2(e, g, tp, par=par):
                return e.tensor_copy(
                    ln2tmp[par].ap().rearrange("p (j c) -> p j c", j=ND)
                    [:, g * 4:g * 4 + 4, :],
                    tp.ap().rearrange("p (a b) -> p a b", a=4)[:, :, :])
            trw = [nd] + ([ln2tmp_use[par]] if ln2tmp_use[par] else [])
            tr = emit_transpose16(ln2b[par], cp2, trw, identb, mm2)
            ln_tp.append(tr)
            p.dma("gpsimd",
                  bass.AP(ln2t_t, sc4 * DM * 512 + qq * 128,
                          [[512, 128], [128 * 512, 16], [1, 128]]),
                  ln2tmp[par][:], [tr], d_ln2[par])
            ln2tmp_use[par] = (d_ln2[par].h, d_ln2[par].n)
            f0_steps(4)
        ln2_all = [(d_ln2[0].h, d_ln2[0].n), (d_ln2[1].h, d_ln2[1].n)]
        x1_done = ln_norm[-1]
        for i in range(4):
            p.dma("gpsimd", out_h[i * 512:(i + 1) * 512, :],
                  x1_h[i * 512:(i + 1) * 512, :], list(acc_all), d_outp)
        outp_all = (d_outp.h, d_outp.n)
        f0_steps(192)
        def tq_finalize2(w_i, acc_col, nw, want_scale_bcast, psum):
            p.op("tensor", (lambda e: e.matmul(
                psum[0:1, 0:1], tq_acc[:, acc_col:acc_col + 1], onescf[:],
                start=True, stop=True)),
                [(vself.h, vself.n), (sself.h, sself.n), (vv.h, vv.n),
                 (mm2.h, mm2.n), (ss.h, ss.n)], mm2)
            p.op("vector", (lambda e, w=w_i, nw=nw: e.tensor_scalar(
                tq_row[:, w:w + 1], psum[0:1, 0:1], 1.0 / nw, EPS_TQ,
                ALU.mult, ALU.add)), [(mm2.h, mm2.n)], vself)
            p.op("vector", (lambda e, w=w_i: e.reciprocal(
                tq_row[:, 4 + w:5 + w], tq_row[:, w:w + 1])),
                [(vself.h, vself.n)], vself)
            p.op("tensor", (lambda e, w=w_i: e.matmul(
                psum[:, 0:1], onesrf[:], tq_row[0:1, 4 + w:5 + w],
                start=True, stop=True)), [(vself.h, vself.n)], mm2)
            p.op("scalar", (lambda e, w=w_i: e.copy(tqt[:, w:w + 1],
                                                    psum[:, 0:1])),
                 [(mm2.h, mm2.n)], ss)
            if want_scale_bcast:
                p.op("tensor", (lambda e, w=w_i: e.matmul(
                    psum[:, 128:129], onesrf[:], tq_row[0:1, w:w + 1],
                    start=True, stop=True)), [(ss.h, ss.n)], mm2)
                p.op("scalar", (lambda e: e.copy(tqt[:, 3:4],
                                                 psum[:, 128:129])),
                     [(mm2.h, mm2.n)], ss)

        tq_finalize2(0, 0, NW_UG, True, tps2[0])
        tq_finalize2(1, 1, NW_UG, False, tps2[0])
        tq_finalize2(2, 2, NW_D, False, tps2[0])
        tq_hold["ug"] = (ss.h, ss.n)
        p.op("vector", (lambda e: e.tensor_mul(
            tqt[:, 4:5], tqt[:, 1:2], tqt[:, 2:3])),
            [(vself.h, vself.n), (ss.h, ss.n)], vself)
        p.op("vector", (lambda e: e.reciprocal(
            tqt[:, 4:5], tqt[:, 4:5])), [(vself.h, vself.n)], vself)
        p.op("scalar", (lambda e: e.copy(tq_row[:, 7:8], tqt[0:1, 4:5])),
             [(vself.h, vself.n)], ss)
        tq_hold["sgd"] = (ss.h, ss.n)
        esLp.close()
        esL.close()
        esF0.close()

        if "x1" in dbg:
            d = dbg_out("x1", [S, DM], F32)
            p.dma("gpsimd", d[:], x1_h[:], list(acc_all), c_ld)

    if stage_n >= 6:
        p.barrier(bar)
        esY.close()
        esF = contextlib.ExitStack()
        h_sb = sb_in(esF, "h_sb", [128, DE * 8], BF16)
        fwraw = [sb_in(esF, f"fwraw{i}", [128, 1024], F32) for i in range(4)]
        tern = [sb_in(esF, f"tern{i}", [128, DM], BF16) for i in range(2)]
        def tq_finalize(w_i, acc_col, nw, want_scale_bcast, psum, extra_wait=()):
            p.op("tensor", (lambda e: e.matmul(
                psum[0:1, 0:1], tq_acc[:, acc_col:acc_col + 1], onescf[:],
                start=True, stop=True)),
                [(vself.h, vself.n), (ss.h, ss.n)] + list(extra_wait), mm2)
            p.op("vector", (lambda e, w=w_i, nw=nw: e.tensor_scalar(
                tq_row[:, w:w + 1], psum[0:1, 0:1], 1.0 / nw, EPS_TQ,
                ALU.mult, ALU.add)), [(mm2.h, mm2.n)], vself)
            p.op("vector", (lambda e, w=w_i: e.reciprocal(
                tq_row[:, 4 + w:5 + w], tq_row[:, w:w + 1])),
                [(vself.h, vself.n)], vself)
            p.op("tensor", (lambda e, w=w_i: e.matmul(
                psum[:, 0:1], onesrf[:], tq_row[0:1, 4 + w:5 + w],
                start=True, stop=True)), [(vself.h, vself.n)], mm2)
            p.op("scalar", (lambda e, w=w_i: e.copy(tqt[:, w:w + 1],
                                                    psum[:, 0:1])),
                 [(mm2.h, mm2.n)], ss)
            if want_scale_bcast:
                p.op("tensor", (lambda e, w=w_i: e.matmul(
                    psum[:, 128:129], onesrf[:], tq_row[0:1, w:w + 1],
                    start=True, stop=True)), [(ss.h, ss.n)], mm2)
                p.op("scalar", (lambda e: e.copy(tqt[:, 3:4], psum[:, 128:129])),
                     [(mm2.h, mm2.n)], ss)

        def emit_quant_piece(src_hbm_slice, inv_col, dst_cols, par, tb,
                             extra=()):
            wt = list(extra)
            if fst["fw_use"][par] is not None:
                wt.append(fst["fw_use"][par])
            p.dma("sync", fwraw[par][:], src_hbm_slice, wt, d_fw[par])
            ld = (d_fw[par].h, d_fw[par].n)
            wt = [ld, tq_hold["ug"]]
            if fst["tern_use"][tb] is not None and dst_cols[0] == 0:
                wt.append(fst["tern_use"][tb])
            p.op("vector", (lambda e, par=par, c=inv_col: e.tensor_scalar(
                fwraw[par][:], fwraw[par][:], tqt[:, c:c + 1], -1.0,
                ALU.mult, ALU.max)), wt, vv)
            p.op("vector", (lambda e, par=par: e.tensor_scalar(
                fwraw[par][:], fwraw[par][:], 1.0, RC, ALU.min, ALU.add)),
                (), vv)
            p.op("vector", (lambda e, par=par, d=dst_cols, tb=tb:
                 e.tensor_scalar(tern[tb][:, d[0]:d[1]], fwraw[par][:], RC,
                                 None, ALU.subtract)), (), vv)
            qd = (vv.h, vv.n)
            fst["fw_use"][par] = qd
            return qd

        fst = {"fw_use": [None] * 4, "tern_use": [None, None]}
        import os as _os
        _nhf = int(_os.environ.get("F_HALVES", "2"))
        _skipf2 = _os.environ.get("F_SKIP_F2", "0") == "1"
        for hf in range(_nhf):
            esF1 = contextlib.ExitStack()
            lnh = sb_in(esF1, f"lnh{hf}", [128, ND * 1024], BF16)
            wut_sb = sb_in(esF1, f"wutsb{hf}", [128, DM], BF16)
            wgt_sb = sb_in(esF1, f"wgtsb{hf}", [128, DM], BF16)
            silu_sb = sb_in(esF1, f"silu{hf}", [128, 1024], BF16)
            esF1p = contextlib.ExitStack()
            up_ps = ps_in(esF1p, f"up_ps{hf}", [128, 1024])
            gate_ps = ps_in(esF1p, f"gate_ps{hf}", [128, 1024])
            tpf = [ps_in(esF1p, f"tpf{hf}_{i}", [128, 512]) for i in range(2)]
            tps = tpf
            _fin_guard = tq_hold["ug"]
            tq_rdy_ug = (tq_hold["ug"], tq_hold["ug"])
            f1 = {"wut_use": None, "wgt_use": None, "up_use": _fin_guard,
                  "gate_use": None, "silu_use": _fin_guard}
            for j in range(ND):
                for c2 in range(2):
                    sc4 = hf * 2 + c2
                    p.dma("sync", lnh[:, j * 1024 + c2 * 512:
                                      j * 1024 + (c2 + 1) * 512],
                          bass.AP(ln2t_t, sc4 * DM * 512 + j * 128 * 512,
                                  [[512, 128], [1, 512]]),
                          list(ln2_all) + [(bar.h, bar.n)], d_ln2l)
            lnh_rdy = (d_ln2l.h, d_ln2l.n)
            pend_hTT = None

            def flush_hTT():
                nonlocal pend_hTT
                if pend_hTT is not None:
                    pec, pw = pend_hTT
                    p.op("vector", (lambda e, ec=pec, g2=gate_ps,
                                    s2=silu_sb: e.tensor_mul(
                        h_sb[:, ec * 1024:(ec + 1) * 1024], g2[:],
                        s2[:])), pw, vv)
                    f1["gate_use"] = (vv.h, vv.n)
                    f1["silu_use"] = (vv.h, vv.n)
                    pend_hTT = None

            def emit_proj(ps_t, w_sb, rdy, use_key, tgen=None):
                tr = None
                for j in range(ND):
                    for nn in range(2):
                        wt = []
                        if j == 0 and nn == 0:
                            wt = [rdy, lnh_rdy]
                            if f1[use_key] is not None:
                                wt.append(f1[use_key])
                        p.op("tensor", (lambda e, j=j, nn=nn, ps_t=ps_t,
                                        w=w_sb, lnh=lnh: e.matmul(
                            ps_t[:, nn * 512:(nn + 1) * 512],
                            w[:, j * 128:(j + 1) * 128],
                            lnh[:, j * 1024 + nn * 512:
                                j * 1024 + (nn + 1) * 512],
                            start=(j == 0), stop=(j == ND - 1))),
                            wt, mm if (j == ND - 1 and nn == 1) else None)
                    if tgen is not None and j in (8, 10, 12, 14):
                        n2 = next(tgen, None)
                        if n2 is not None:
                            tr = n2
                return (mm.h, mm.n), tr

            def quant_u(ec2):
                qd = None
                for piece in range(2):
                    qd = emit_quant_piece(
                        wu_h[ec2 * 128:(ec2 + 1) * 128,
                             piece * 1024:(piece + 1) * 1024],
                        0, (piece * 1024, (piece + 1) * 1024), piece, 0)
                return qd

            def quant_g(ec2):
                qd = None
                for piece in range(2):
                    qd = emit_quant_piece(
                        wg_h[ec2 * 128:(ec2 + 1) * 128,
                             piece * 1024:(piece + 1) * 1024],
                        1, (piece * 1024, (piece + 1) * 1024), 2 + piece, 1)
                return qd

            def cpu_(e, g, tp, d=wut_sb):
                return e.tensor_copy(d[:, g * 512:(g + 1) * 512], tp[:])

            def cpg_(e, g, tp, d=wgt_sb):
                return e.tensor_copy(d[:, g * 512:(g + 1) * 512], tp[:])

            tpu_gen = None
            tpg_args = None
            tr_u_next = None
            for ec in range(NE):
                if hf == 0:
                    if ec == 0:
                        qd_u = quant_u(0)
                        gw = [qd_u, (d_tc[0].h, d_tc[0].n)] + (
                            [f1["wut_use"]] if f1["wut_use"] else [])
                        rdy_u = emit_transpose16(tern[0], cpu_, gw, identb,
                                                 mm2)
                        fst["tern_use"][0] = rdy_u
                        p.dma("gpsimd",
                              bass.AP(wut_t, 0, [[DM, 128], [1, DM]]),
                              wut_sb[:], [rdy_u], d_tc[0])
                        qd_g = quant_g(0)
                        gw = [qd_g, (d_tc[1].h, d_tc[1].n)] + (
                            [f1["wgt_use"]] if f1["wgt_use"] else [])
                        rdy_g = emit_transpose16(tern[1], cpg_, gw, identb,
                                                 mm2)
                        fst["tern_use"][1] = rdy_g
                        p.dma("gpsimd",
                              bass.AP(wgt_t, 0, [[DM, 128], [1, DM]]),
                              wgt_sb[:], [rdy_g], d_tc[1])
                        tpg_gen = None
                    else:
                        rdy_u = tr_u_next
                        qd_g_s, d1cnt = tpg_args
                        gw = [qd_g_s, d1cnt] + (
                            [f1["wgt_use"]] if f1["wgt_use"] else [])
                        tpg_gen = emit_transpose16_gen(tern[1], cpg_, gw,
                                                       identb, mm2)
                    flush_hTT()
                    up_mm, tr_g = emit_proj(up_ps, wut_sb, rdy_u, "up_use",
                                            tpg_gen)
                    if ec > 0:
                        rdy_g = tr_g
                        fst["tern_use"][1] = tr_g
                        p.dma("gpsimd",
                              bass.AP(wgt_t, ec * 128 * DM,
                                      [[DM, 128], [1, DM]]),
                              wgt_sb[:], [tr_g], d_tc[1])
                    f1["wut_use"] = up_mm
                    wt = [up_mm]
                    if f1["silu_use"] is not None:
                        wt.append(f1["silu_use"])
                    p.op("scalar", (lambda e, silu_sb=silu_sb, u=up_ps:
                         e.activation(silu_sb[:], u[:], AF.Silu,
                                      scale=tqt[:, 3:4])), wt, ss)
                    f1["up_use"] = (ss.h, ss.n)
                    if ec + 1 < NE:
                        qd_u1 = quant_u(ec + 1)
                        qd_g1 = quant_g(ec + 1)
                        tpg_args = (qd_g1, (d_tc[1].h, d_tc[1].n))
                        gw = [qd_u1, (d_tc[0].h, d_tc[0].n)] + (
                            [f1["wut_use"]] if f1["wut_use"] else [])
                        tpu_gen = emit_transpose16_gen(tern[0], cpu_, gw,
                                                       identb, mm2)
                    else:
                        tpu_gen = None
                    gate_mm, tr_u2 = emit_proj(gate_ps, wgt_sb, rdy_g,
                                               "gate_use", tpu_gen)
                    if tpu_gen is not None:
                        tr_u_next = tr_u2
                        fst["tern_use"][0] = tr_u2
                        p.dma("gpsimd",
                              bass.AP(wut_t, (ec + 1) * 128 * DM,
                                      [[DM, 128], [1, DM]]),
                              wut_sb[:], [tr_u2], d_tc[0])
                    f1["wgt_use"] = gate_mm
                    pend_hTT = (ec, [(ss.h, ss.n), gate_mm])
                    continue
                flush_hTT()
                wt = [(d_tc[0].h, d_tc[0].n)]
                if f1["wut_use"] is not None:
                    wt.append(f1["wut_use"])
                p.dma("sync", wut_sb[:],
                      bass.AP(wut_t, ec * 128 * DM, [[DM, 128], [1, DM]]),
                      wt, d_tcl[0])
                rdy_u = (d_tcl[0].h, d_tcl[0].n)
                wt = [(d_tc[1].h, d_tc[1].n)]
                if f1["wgt_use"] is not None:
                    wt.append(f1["wgt_use"])
                p.dma("sync", wgt_sb[:],
                      bass.AP(wgt_t, ec * 128 * DM, [[DM, 128], [1, DM]]),
                      wt, d_tcl[1])
                rdy_g = (d_tcl[1].h, d_tcl[1].n)
                up_mm, _ = emit_proj(up_ps, wut_sb, rdy_u, "up_use")
                gate_mm, _ = emit_proj(gate_ps, wgt_sb, rdy_g, "gate_use")
                f1["wut_use"] = up_mm
                f1["wgt_use"] = gate_mm
                wt = [up_mm]
                if f1["silu_use"] is not None:
                    wt.append(f1["silu_use"])
                p.op("scalar", (lambda e, silu_sb=silu_sb, u=up_ps:
                     e.activation(silu_sb[:], u[:], AF.Silu,
                                  scale=tqt[:, 3:4])), wt, ss)
                f1["up_use"] = (ss.h, ss.n)
                pend_hTT = (ec, [(ss.h, ss.n), gate_mm])
            if pend_hTT is not None:
                pec, pw = pend_hTT
                p.op("vector", (lambda e, ec=pec, g2=gate_ps, s2=silu_sb:
                     e.tensor_mul(h_sb[:, ec * 1024:(ec + 1) * 1024], g2[:],
                                  s2[:])), pw, vv)
                pend_hTT = None
            h_rdy = (vv.h, vv.n)
            sgd_rdy = tq_hold["sgd"]
            if "h2" in dbg and hf == 1:
                d4 = dbg_out("h2", [128, DE * 8], BF16)
                p.dma("gpsimd", d4[:], h_sb[:], [h_rdy], c_ld)
                d6 = dbg_out("wdt0", [128, DE], BF16)
                p.dma("gpsimd", d6[:], bass.AP(wdt_t, 0, [[DE, 128], [1, DE]]),
                      [h_rdy], c_ld)
                d7 = dbg_out("wdt15", [128, DE], BF16)
                p.dma("gpsimd", d7[:], bass.AP(wdt_t, 15 * 128 * DE,
                                               [[DE, 128], [1, DE]]),
                      [h_rdy], c_ld)
            if "tq" in dbg and hf == 0:
                d1 = dbg_out("tqt", [128, 8], F32)
                p.dma("gpsimd", d1[:], tqt[:], [sgd_rdy], c_ld)
                d2 = dbg_out("tqrow", [1, 8], F32)
                p.dma("gpsimd", d2[:], tq_row[:], [sgd_rdy], c_ld)
                d3 = dbg_out("wut0", [128, DM], BF16)
                p.dma("gpsimd", d3[:], bass.AP(wut_t, 0, [[DM, 128], [1, DM]]),
                      [(d_tc[0].h, d_tc[0].n)], c_ld)
            if "h" in dbg and hf == 0:
                d = dbg_tensors.get("h") or dbg_out("h", [128, DE * 8], BF16)
                p.dma("gpsimd", d[:], h_sb[:], [h_rdy], c_ld)
            esF1p.close()
            esF1.close()
            if _skipf2:
                continue

            esF2 = contextlib.ExitStack()
            nwd = 1 if hf == 0 else 2
            wdt_l = [sb_in(esF2, f"wdtsb{hf}_{i}", [128, DE], BF16)
                     for i in range(nwd)]
            wdt_sb = wdt_l[0]
            f2tmp = [sb_in(esF2, f"f2tmp{hf}_{i}", [128, 512], F32)
                     for i in range(2)]
            f2out = [sb_in(esF2, f"f2out{hf}_{i}", [128, 512], F32)
                     for i in range(2)]
            esF2p = contextlib.ExitStack()
            f2_ps = [ps_in(esF2p, f"f2ps{hf}_{i}", [128, 512]) for i in range(2)]
            f2t_ps = [ps_in(esF2p, f"f2tps{hf}_{i}", [128, 512])
                      for i in range(2)]
            tps_prev, tps = tps, f2t_ps
            f2 = {"wdt_use": [None] * nwd, "ps_use": [None, None],
                  "tps_use": [None, None], "tmp_use": [None, None],
                  "out_use": [None, None]}
            f2_pend = []

            def _flush_f2(item):
                gq, m, par, evac_w = item
                for qq in range(4):
                    wt = []
                    if qq == 0:
                        wt = [evac_w, (vv.h, vv.n)]
                        if f2["tps_use"][par] is not None:
                            wt.append(f2["tps_use"][par])
                    p.op("tensor", (lambda e, qq=qq, par=par, tp2=f2t_ps,
                                    tl=f2tmp: e.matmul(
                        tp2[par][:, qq * 128:(qq + 1) * 128],
                        tl[par][:, qq * 128:(qq + 1) * 128],
                        identf[:], start=True, stop=True)),
                        wt, mm2 if qq == 3 else None)
                f2["tmp_use"][par] = (mm2.h, mm2.n)
                wt = [(mm2.h, mm2.n)]
                if f2["out_use"][par] is not None:
                    wt.append(f2["out_use"][par])
                p.op("scalar", (lambda e, par=par, ol=f2out, tp2=f2t_ps:
                     e.copy(ol[par][:], tp2[par][:])), wt, ss)
                f2["tps_use"][par] = (ss.h, ss.n)
                p.dma("gpsimd",
                      bass.AP(out_h.tensor, gq * 512 * DM + m * 128,
                              [[DM, 128], [128 * DM, 4], [1, 128]]),
                      f2out[par][:], [(ss.h, ss.n), outp_all], d_fo[par],
                      accum_op=ALU.add)
                f2["out_use"][par] = (d_fo[par].h, d_fo[par].n)
            for m in range(ND):
                wd_b = m % nwd
                wdt_sb = wdt_l[wd_b]
                if hf == 0:
                    for piece in range(8):
                        tb = (piece // 2) % 2
                        qd = emit_quant_piece(
                            wd_h[m * 128:(m + 1) * 128,
                                 piece * 1024:(piece + 1) * 1024],
                            2, ((piece % 2) * 1024, (piece % 2) * 1024 + 1024),
                            piece % 4, tb, extra=[sgd_rdy, h_rdy])
                        if piece % 2 == 1:
                            pp = piece // 2

                            def cpd_(e, g, tp, pp=pp, w=wdt_sb):
                                return e.tensor_copy(
                                    w[:, pp * 2048 + g * 512:
                                      pp * 2048 + (g + 1) * 512], tp[:])
                            gw = [qd]
                            if pp == 0:
                                gw.append((d_tc[2].h, d_tc[2].n))
                                if m == 0:
                                    gw.append((d_tc[0].h, d_tc[0].n))
                                    gw.append((d_tc[1].h, d_tc[1].n))
                                if f2["wdt_use"][wd_b]:
                                    gw.append(f2["wdt_use"][wd_b])
                                for pr in range(2):
                                    if f2["tps_use"][pr]:
                                        gw.append(f2["tps_use"][pr])
                            tr = emit_transpose16(tern[tb], cpd_, gw,
                                                  identb, mm2)
                            fst["tern_use"][tb] = tr
                    wdt_rdy = (vv.h, vv.n)
                    p.dma("gpsimd",
                          bass.AP(wdt_t, m * 128 * DE, [[DE, 128], [1, DE]]),
                          wdt_sb[:], [wdt_rdy], d_tc[2])
                else:
                    wt = [(d_tc[2].h, d_tc[2].n), h_rdy]
                    if f2["wdt_use"][wd_b] is not None:
                        wt.append(f2["wdt_use"][wd_b])
                    p.dma("sync", wdt_sb[:],
                          bass.AP(wdt_t, m * 128 * DE, [[DE, 128], [1, DE]]),
                          wt, d_tcl[m % 2])
                    wdt_rdy = (d_tcl[m % 2].h, d_tcl[m % 2].n)
                for ss2 in range(2):
                    gq = hf * 2 + ss2
                    par = (2 * m + ss2) % 2
                    for ec in range(NE):
                        wt = []
                        if ec == 0:
                            wt = [wdt_rdy, h_rdy, sgd_rdy]
                            if f2["ps_use"][par] is not None:
                                wt.append(f2["ps_use"][par])
                        p.op("tensor", (lambda e, ec=ec, ss2=ss2, par=par,
                                        w=wdt_sb:
                             e.matmul(f2_ps[par][:],
                                      w[:, ec * 128:(ec + 1) * 128],
                                      h_sb[:, ec * 1024 + ss2 * 512:
                                           ec * 1024 + (ss2 + 1) * 512],
                                      start=(ec == 0), stop=(ec == NE - 1))),
                             wt, mm if ec == NE - 1 else None)
                    f2["wdt_use"][wd_b] = (mm.h, mm.n)
                    wt = [(mm.h, mm.n)]
                    if f2["tmp_use"][par] is not None:
                        wt.append(f2["tmp_use"][par])
                    p.op("scalar", (lambda e, par=par, tl=f2tmp, pl=f2_ps:
                         e.activation(tl[par][:], pl[par][:], AF.Identity,
                                      scale=tqt[:, 4:5])), wt, ss)
                    f2["ps_use"][par] = (ss.h, ss.n)
                    f2_pend.append((gq, m, par, (ss.h, ss.n)))
                    if len(f2_pend) >= 2:
                        _flush_f2(f2_pend.pop(0))
            while f2_pend:
                _flush_f2(f2_pend.pop(0))
            esF2p.close()
            esF2.close()
            tps = tps_prev
            if hf == 0:
                p.barrier(bar)
        esF.close()

    gw = [(c.h, c.n) for c in
          (d_acc[0], d_acc[1], d_outp, d_fo[0], d_fo[1], d_wot[0], d_wot[1],
           d_ln2[0], d_ln2[1], d_tc[0], d_tc[1], d_tc[2], d_x1p, d_sw, c_ld)]
    p.wait("gpsimd", gw)
    sw = [(c.h, c.n) for c in (d_sc[0], d_sc[1], d_sc[2], d_sc[3],
                               d_ln2l, d_fw[0], d_fw[1], d_fw[2], d_fw[3],
                               d_tcl[0], d_tcl[1], d_wotl[0], d_wotl[1],
                               d_x1l[0], d_x1l[1])]
    p.wait("sync", sw)

    with nc.Block() as block:
        p.emit(block)
    if stage_n < 5:
        if stage_n < 4:
            esTP.close()
        esA.close()
    if stage_n < 6:
        esY.close()
    es.close()
    return nc, dbg_tensors


def _consts():
    half = HD // 2
    inv_freq = 10000.0 ** (-np.arange(0, half, dtype=np.float32) / half)
    freqs = np.arange(S, dtype=np.float32)[:, None] * inv_freq[None, :]
    cos = np.cos(freqs).astype(np.float32)
    sin = np.sin(freqs).astype(np.float32)
    cosf = np.concatenate([cos.T, cos.T], axis=0).astype(ml_dtypes.bfloat16)
    sinf = np.concatenate([sin.T, sin.T], axis=0).astype(ml_dtypes.bfloat16)
    kq = np.arange(128)
    maskd = np.where(kq[None, :] >= kq[:, None], 0.0,
                     -1e38).astype(ml_dtypes.bfloat16)
    maskf = np.where(kq[None, :] < kq[:, None], 0.0,
                     -1e38).astype(ml_dtypes.bfloat16)
    sign = np.ones((128, 1), np.float32)
    sign[64:] = -1.0
    return {
        "identb": np.eye(128, dtype=ml_dtypes.bfloat16),
        "identf": np.eye(128, dtype=np.float32),
        "cosf": cosf, "sinf": sinf, "maskd": maskd, "maskf": maskf,
        "onescb": np.ones((128, 1), ml_dtypes.bfloat16),
        "onescf": np.ones((128, 1), np.float32),
        "onesrf": np.ones((1, 128), np.float32),
        "sign": sign,
        "epsc": np.full((128, 1), EPS_RMS, np.float32),
    }


def _in_maps(L0_attn_x, L0_attn_Wq, L0_attn_Wk, L0_attn_Wv, L0_attn_Wo,
             L0_attn_q_gain, L0_ffn_W_up, L0_ffn_W_gate, L0_ffn_W_down):
    consts = _consts()
    in_maps = []
    for b in range(8):
        m = dict(consts)
        m["x"] = np.ascontiguousarray(L0_attn_x[b])
        m["wq"] = L0_attn_Wq
        m["wk"] = L0_attn_Wk
        m["wv"] = L0_attn_Wv
        m["wo"] = L0_attn_Wo
        m["gain"] = np.ascontiguousarray(L0_attn_q_gain.reshape(1, H))
        m["wu"] = L0_ffn_W_up
        m["wg"] = L0_ffn_W_gate
        m["wd"] = L0_ffn_W_down
        in_maps.append(m)
    return in_maps


def kernel(L0_attn_x, L0_attn_Wq, L0_attn_Wk, L0_attn_Wv, L0_attn_Wo,
           L0_attn_q_gain, L0_ffn_W_up, L0_ffn_W_gate, L0_ffn_W_down):
    nc, _ = build_nc()
    in_maps = _in_maps(L0_attn_x, L0_attn_Wq, L0_attn_Wk, L0_attn_Wv,
                       L0_attn_Wo, L0_attn_q_gain, L0_ffn_W_up,
                       L0_ffn_W_gate, L0_ffn_W_down)
    res = run_bass_kernel_spmd(nc, in_maps, core_ids=list(range(8)))
    return np.stack([res.results[b]["out"] for b in range(8)], axis=0)
